# version 10
# baseline (speedup 1.0000x reference)
import contextlib

import numpy as np
import ml_dtypes

import concourse.bass as bass
import concourse.mybir as mybir
from concourse.bass_utils import run_bass_kernel_spmd

F32 = mybir.dt.float32
BF16 = mybir.dt.bfloat16
AF = mybir.ActivationFunctionType
ALU = mybir.AluOpType
BF = ml_dtypes.bfloat16

B, L, D = 2, 4096, 2048
R, DH, HID = 2048, 512, 8192
DC, RC, HC, GC = 16, 16, 4, 64
W = 4096
TA = 512
NTA = W // TA
OWN = 1024
TB = 256
NTB = OWN // TB
TC = 512
NTC = OWN // TC
T0 = 342
XO_W = OWN + 2
EPS = 1e-5

A0, A1, B0, B1, B2, CP, CM = range(7)


class Em:
    STREAMS = ("pe", "dve", "act", "sp", "pool")

    def __init__(self):
        self.items = {s: [] for s in self.STREAMS}
        self.opcnt = {s: 0 for s in ("pe", "dve", "act", "pool")}
        self.dmacnt = {"ds": 0, "dp": 0}

    def _wait(self, stream, tok):
        if tok is None:
            return
        self.items[stream].append(("wait", tok[0], tok[1]))

    def op(self, stream, fn, waits=()):
        for t in waits:
            self._wait(stream, t)
        self.opcnt[stream] += 1
        v = self.opcnt[stream]
        self.items[stream].append(("op", fn, stream, v))
        return (stream, v)

    def dma(self, stream, fn, waits=()):
        for t in waits:
            self._wait(stream, t)
        sem = "ds" if stream == "sp" else "dp"
        self.dmacnt[sem] += 16
        v = self.dmacnt[sem]
        self.items[stream].append(("dma", fn, sem, v))
        return (sem, v)

    def scope(self, name):
        for stream in self.STREAMS:
            self.items[stream].append(("scope", name, 0))

    def barrier(self):
        toks = [(s, self.opcnt[s]) for s in ("pe", "dve", "act", "pool")]
        toks += [("ds", self.dmacnt["ds"]), ("dp", self.dmacnt["dp"])]
        for stream in self.STREAMS:
            for t in toks:
                self._wait(stream, t)

    def finalize(self):
        targets = {"pe": set(), "dve": set(), "act": set(), "pool": set()}
        for s in self.STREAMS:
            for it in self.items[s]:
                if it[0] == "wait" and it[1] in targets and it[2] > 0:
                    targets[it[1]].add(it[2])
        remap = {}
        for sem, tset in targets.items():
            marked = sorted(tset)
            remap[sem] = (set(marked), {v: i + 1 for i, v in enumerate(marked)})
        out = {s: [] for s in self.STREAMS}
        for s in self.STREAMS:
            last = {}
            for it in self.items[s]:
                if it[0] == "wait":
                    sem, val = it[1], it[2]
                    if sem in remap and val > 0:
                        val = remap[sem][1][val]
                    if last.get(sem, -1) >= val:
                        continue
                    last[sem] = val
                    out[s].append(("wait", sem, val))
                elif it[0] == "op":
                    _, fn, sem, v = it
                    inc = v in remap[sem][0]
                    out[s].append(("op", fn, sem if inc else None, 0))
                else:
                    out[s].append(it)
        self.items = out
        self._simulate()

    def _simulate(self):
        counts = {"pe": 0, "dve": 0, "act": 0, "pool": 0, "ds": 0, "dp": 0}
        pos = {s: 0 for s in self.STREAMS}
        progressed = True
        while progressed:
            progressed = False
            for s in self.STREAMS:
                while pos[s] < len(self.items[s]):
                    it = self.items[s][pos[s]]
                    if it[0] == "wait":
                        if counts[it[1]] >= it[2]:
                            pos[s] += 1
                            progressed = True
                        else:
                            break
                    elif it[0] == "op":
                        if it[2] is not None:
                            counts[it[2]] += 1
                        pos[s] += 1
                        progressed = True
                    elif it[0] == "scope":
                        pos[s] += 1
                        progressed = True
                    else:
                        counts[it[2]] = max(counts[it[2]], it[3])
                        pos[s] += 1
                        progressed = True
        stuck = {s: (pos[s], len(self.items[s]), self.items[s][pos[s]][:3])
                 for s in self.STREAMS if pos[s] < len(self.items[s])}
        if stuck:
            raise RuntimeError(f"emitter deadlock: {stuck} counts={counts}")


class Buf:
    def __init__(self):
        self.writer = None
        self.readers = []


class Slab:
    def __init__(self, nelem):
        self.cap = nelem
        self.placed = []
        self.peak = 0

    def alloc(self, nelem, phases, align=32):
        phases = set(phases)
        cands = {0}
        for off, size, ph in self.placed:
            cands.add(-(-(off + size) // align) * align)
        for off in sorted(cands):
            if off + nelem > self.cap:
                continue
            ok = True
            for o2, s2, ph in self.placed:
                if (ph & phases) and not (off + nelem <= o2 or o2 + s2 <= off):
                    ok = False
                    break
            if ok:
                self.placed.append((off, nelem, phases))
                self.peak = max(self.peak, off + nelem)
                return off
        raise RuntimeError(f"slab overflow: need {nelem} for phases {phases}")


def _tile_pd(mat, kc, ic):
    K, I = mat.shape
    assert K == kc * 128 and I == ic * 128
    return np.ascontiguousarray(
        mat.reshape(kc, 128, ic, 128).transpose(1, 0, 2, 3).reshape(128, kc * ic * 128))


def _stripes_i(mat, kc, ic):
    K, I = mat.shape
    assert K == kc * 128 and I == ic * 128
    return np.ascontiguousarray(
        mat.reshape(kc, 128, ic, 128).transpose(2, 1, 0, 3).reshape(ic, 128, kc * 128))


def _col(vec, n):
    return np.ascontiguousarray(np.asarray(vec, np.float32).reshape(n, 128).T)


def make_in_maps(inputs):
    x = np.asarray(inputs["L0_attn_x"], dtype=np.float32)
    w_gate = np.asarray(inputs["L0_ffn_w_gate"], dtype=np.float32)
    w_up = np.asarray(inputs["L0_ffn_w_up"], dtype=np.float32)
    conv_w = np.asarray(inputs["L0_attn_conv_w"], dtype=np.float32)

    wg_s = _stripes_i(np.ascontiguousarray(w_gate.T), DC, GC)
    wu_s = _stripes_i(np.ascontiguousarray(w_up.T), DC, GC)

    shared = dict(
        wuT=_tile_pd(np.ascontiguousarray(np.asarray(inputs["L0_attn_Wu"], np.float32).T), DC, RC),
        wg1T=_tile_pd(np.ascontiguousarray(np.asarray(inputs["L0_attn_Wg1"], np.float32).T), DC, HC),
        wg2T=_tile_pd(np.ascontiguousarray(np.asarray(inputs["L0_attn_Wg2"], np.float32).T), HC, RC),
        wvT=_stripes_i(np.ascontiguousarray(np.asarray(inputs["L0_attn_Wv"], np.float32).T), DC, RC).reshape(RC // 2, 2, 128, DC * 128).transpose(0, 2, 1, 3).reshape(RC // 2, 128, 2 * DC * 128),
        wgu=np.ascontiguousarray(np.stack([wg_s, wu_s], axis=1).transpose(0, 2, 1, 3).reshape(GC, 128, 2 * DC * 128)),
        wdT=_stripes_i(np.ascontiguousarray(np.asarray(inputs["L0_ffn_w_down"], np.float32).T), GC, DC),
        bu=_col(inputs["L0_attn_bu"], RC),
        bv=_col(inputs["L0_attn_bv"], RC),
        bg1=_col(inputs["L0_attn_bg1"], HC),
        bg2=_col(inputs["L0_attn_bg2"], RC),
        cb=_col(inputs["L0_attn_conv_b"], DC),
        cw=np.ascontiguousarray(
            conv_w.reshape(DC, 128, 3).transpose(1, 0, 2).reshape(128, DC * 3)),
        ones_bf=np.ones((128, 1), dtype=BF),
        ones_row=np.ones((1, 128), dtype=np.float32),
        consts=np.array([[1.0 / D, EPS, 0.0, 0.0]], dtype=np.float32),
    )

    in_maps = []
    for c in range(8):
        b, p = c // 4, c % 4
        xbT = np.ascontiguousarray(x[b].T)
        npad = 3072 - 1024 * p
        xfed = np.zeros((D, W), dtype=np.float32)
        if p > 0:
            xfed[:, npad:3072] = xbT[:, : 1024 * p]
        xfed[:, 3072:4096] = xbT[:, 1024 * p: 1024 * (p + 1)]
        xf_t = np.ascontiguousarray(
            xfed.reshape(DC, 128, NTA, TA).transpose(2, 1, 0, 3)
            .reshape(NTA, 128, DC * TA))
        mask = np.zeros((W,), dtype=np.float32)
        mask[npad:] = 1.0
        mask_t = np.ascontiguousarray(
            np.broadcast_to(mask.reshape(NTA, 1, TA), (NTA, 128, TA))).astype(BF)
        xo = np.zeros((D, XO_W), dtype=np.float32)
        xo[:, 1:1 + OWN] = xbT[:, 1024 * p: 1024 * (p + 1)]
        if p > 0:
            xo[:, 0] = xbT[:, 1024 * p - 1]
        if p < 3:
            xo[:, 1 + OWN] = xbT[:, 1024 * (p + 1)]
        xo_t = np.ascontiguousarray(
            xo.reshape(DC, 128, XO_W).transpose(1, 0, 2).reshape(128, DC * XO_W))
        m = dict(shared)
        m["xfed"] = xf_t
        m["mask"] = mask_t
        m["xown"] = xo_t
        in_maps.append(m)
    return in_maps


def build_program():
    nc = bass.Bass("TRN2", target_bir_lowering=False)

    d_xfed = nc.dram_tensor("xfed", [NTA, 128, DC * TA], F32, kind="ExternalInput").ap()
    d_mask = nc.dram_tensor("mask", [NTA, 128, TA], BF16, kind="ExternalInput").ap()
    d_xown = nc.dram_tensor("xown", [128, DC * XO_W], F32, kind="ExternalInput").ap()
    d_wuT = nc.dram_tensor("wuT", [128, DC * RC * 128], F32, kind="ExternalInput").ap()
    d_wg1T = nc.dram_tensor("wg1T", [128, DC * HC * 128], F32, kind="ExternalInput").ap()
    d_wg2T = nc.dram_tensor("wg2T", [128, HC * RC * 128], F32, kind="ExternalInput").ap()
    d_wvT = nc.dram_tensor("wvT", [RC // 2, 128, 2 * DC * 128], F32, kind="ExternalInput").ap()
    d_wgu = nc.dram_tensor("wgu", [GC, 128, 2 * DC * 128], F32, kind="ExternalInput").ap()
    d_wdT = nc.dram_tensor("wdT", [DC, 128, GC * 128], F32, kind="ExternalInput").ap()
    d_bu = nc.dram_tensor("bu", [128, RC], F32, kind="ExternalInput").ap()
    d_bv = nc.dram_tensor("bv", [128, RC], F32, kind="ExternalInput").ap()
    d_bg1 = nc.dram_tensor("bg1", [128, HC], F32, kind="ExternalInput").ap()
    d_bg2 = nc.dram_tensor("bg2", [128, RC], F32, kind="ExternalInput").ap()
    d_cb = nc.dram_tensor("cb", [128, DC], F32, kind="ExternalInput").ap()
    d_cw = nc.dram_tensor("cw", [128, DC * 3], F32, kind="ExternalInput").ap()
    d_onesb = nc.dram_tensor("ones_bf", [128, 1], BF16, kind="ExternalInput").ap()
    d_onesr = nc.dram_tensor("ones_row", [1, 128], F32, kind="ExternalInput").ap()
    d_consts = nc.dram_tensor("consts", [1, 4], F32, kind="ExternalInput").ap()

    d_y = nc.dram_tensor("y", [128, DC, OWN], F32, kind="ExternalOutput").ap()
    d_x2 = nc.dram_tensor("x2d", [128, DC, OWN], F32, kind="Internal").ap()
    d_kc = nc.dram_tensor("kcd", [128, RC, OWN], BF16, kind="Internal").ap()
    d_g = nc.dram_tensor("gd", [128, RC, OWN], BF16, kind="Internal").ap()
    d_g1 = nc.dram_tensor("g1d", [128, GC * TC], BF16, kind="Internal").ap()

    em = Em()
    ctx = contextlib.ExitStack()
    with ctx:
        BF_CAP = 77900
        F32_CAP = 3392
        ROW_CAP = 9728
        t_arbf = ctx.enter_context(nc.sbuf_tensor("arbf", [128, BF_CAP], BF16))
        t_arf = ctx.enter_context(nc.sbuf_tensor("arf", [128, F32_CAP], F32))
        t_rows = ctx.enter_context(nc.sbuf_tensor("rows", [1, ROW_CAP], F32))
        slab_bf, slab_f, slab_r = Slab(BF_CAP), Slab(F32_CAP), Slab(ROW_CAP)

        def abf(n, ph):
            off = slab_bf.alloc(n, ph)
            return t_arbf[:, off:off + n]

        def af32(n, ph):
            off = slab_f.alloc(n, ph)
            return t_arf[:, off:off + n]

        def arow(n, ph):
            off = slab_r.alloc(n, ph)
            return t_rows[0:1, off:off + n]

        sb = lambda name, shape, dt: ctx.enter_context(nc.sbuf_tensor(name, shape, dt))

        t_onesb = sb("c_onesb", [128, 1], BF16)
        t_onesr = sb("c_onesr", [1, 128], F32)
        t_consts = sb("c_consts", [1, 4], F32)
        t_bu = sb("c_bu", [128, RC], F32)
        t_bv = sb("c_bv", [128, RC], F32)
        t_bg1 = sb("c_bg1", [128, HC], F32)
        t_bg2 = sb("c_bg2", [128, RC], F32)
        t_cb = sb("c_cb", [128, DC], F32)
        t_cw = sb("c_cw", [128, DC * 3], F32)
        t_carry = sb("c_carry", [128, RC], F32)

        t_wuT = abf(DC * RC * 128, {A0, A1, B0, B1, B2})
        t_xnown = abf(DC * XO_W, {B0, B1, B2})
        t_go = abf(RC * OWN, {B1, B2})
        t_wg1 = abf(DC * HC * 128, {A0, A1})
        t_wg2 = abf(HC * RC * 128, {A0, A1})
        t_xbf = abf(DC * TA, {A0, A1, B0, CP})
        t_xbf2 = abf(DC * TA, {A0, A1, B0, CP})
        t_g = abf(RC * TA, {A1})
        t_h = abf(HC * TA, {A1})
        t_maskt = [abf(TA, {A1}) for _ in range(2)]
        t_sq = abf(DC * TA, {A0, B0, CP})
        t_wv = [abf(2 * DC * 128, {B1}) for _ in range(2)]
        t_kcs = [abf(OWN, {B1}) for _ in range(2)]
        t_gss = [abf(OWN, {B1}) for _ in range(2)]
        t_attn = [abf(DC * TB, {B2}) for _ in range(2)]
        t_sqb2 = abf(DC * TB, {B2})
        t_xn2 = [abf(DC * TC, {CP, CM}) for _ in range(NTC)]
        t_gated = abf(GC * TC, {CM})
        t_wgu = [abf(2 * DC * 128, {CM}) for _ in range(2)]
        t_wd = [abf(32 * 128, {CM}) for _ in range(2)]
        t_g1s = [abf(TC, {CM}) for _ in range(2)]
        t_dbf = abf(DC * TC, {CM})
        t_sqc = abf(TC, {CM})

        t_ug = af32(TA, {A1})
        t_so = [af32(TA, {A1}) for _ in range(3)]
        t_vg = af32(TA, {B1})
        t_cv = [af32(TB, {B2}) for _ in range(2)]
        t_ca = af32(TB, {B2})
        t_cb2 = af32(TB, {B2})
        t_xof = [af32(8 * TB, {B2})] * 2
        t_sg = af32(TC, {CM})
        t_x2c = [af32(4 * TC, {CM})] * 4

        r_m = arow(W, {A0, A1, B0})
        r_r = arow(W, {A0, A1, B0})
        r_v = arow(2 * 512, {A0, B0})
        r_ms = arow(512, {B2, CP, CM})
        r_vs = arow(512, {B2, CP, CM})
        r_rs = arow(512, {B2, CP, CM})
        r_ms2 = arow(512, {B2, CP})
        r_vs2 = arow(512, {B2, CP})
        r_rs2 = arow(512, {B2, CP})
        r_sr = arow(2 * TC * NTC, {CP, CM})

        pb = [ctx.enter_context(nc.psum_tensor(f"pb{i}", [128, 512], F32))
              for i in range(6)]
        pr = [ctx.enter_context(nc.psum_tensor(f"pr{i}", [128, 512], F32))
              for i in range(2)]
        prr = [p[0:1, :] for p in pr]

        bufs = {}

        def Bf(name):
            if name not in bufs:
                bufs[name] = Buf()
            return bufs[name]

        def _rd(names):
            w = []
            for n in names:
                b = Bf(n)
                if b.writer:
                    w.append(b.writer)
            return w

        def _wr(names):
            w = []
            for n in names:
                b = Bf(n)
                w += b.readers
                if b.writer:
                    w.append(b.writer)
            return w

        def _post(tok, reads, writes):
            for n in reads:
                Bf(n).readers.append(tok)
            for n in writes:
                b = Bf(n)
                b.writer = tok
                b.readers = []

        def opc(stream, fn, reads=(), writes=(), wr_first=()):
            waits = _rd(reads) + _wr(writes) + _wr(wr_first)
            tok = em.op(stream, fn, waits=waits)
            _post(tok, reads, writes)
            return tok

        def dma_in(stream, dst_ap, src_ap, dstbuf):
            waits = _wr((dstbuf,))
            q = nc.sync if stream == "sp" else nc.gpsimd
            tok = em.dma(stream, lambda q=q, d=dst_ap, s=src_ap: q.dma_start(d, s),
                         waits=waits)
            _post(tok, (), (dstbuf,))
            return tok

        def dma_out(stream, dst_ap, src_ap, srcbufs):
            waits = _rd(srcbufs)
            q = nc.sync if stream == "sp" else nc.gpsimd
            tok = em.dma(stream, lambda q=q, d=dst_ap, s=src_ap: q.dma_start(d, s),
                         waits=waits)
            _post(tok, srcbufs, ())
            return tok

        def v3(ap, c, t):
            return ap.rearrange("p (c t) -> p c t", c=c)

        def bc3(bank, c, width):
            return pb[bank][:, 0:width].unsqueeze(1).broadcast_to((128, c, width))

        INV_D = t_consts[0:1, 0:1]
        EPS_AP = t_consts[0:1, 1:2]

        def row_stats(src_ap, srcname, width, mrow, vrow, mn, vn,
                      sqa=None, sqn="sq", banks=None, sq_eng="dve"):
            if banks is None:
                bA, bAn, bB, bBn = prr[0], "pr0", prr[1], "pr1"
            else:
                bA, bAn, bB, bBn = banks
            for dc in range(DC):
                opc("pe", (lambda dc=dc, src_ap=src_ap, width=width, bA=bA:
                    nc.tensor.matmul(
                        bA[:, 0:width], t_onesb[:],
                        src_ap[:, dc * width:(dc + 1) * width],
                        start=(dc == 0), stop=(dc == DC - 1))),
                    reads=(srcname, "onesb"),
                    wr_first=(bAn,) if dc == 0 else (),
                    writes=(bAn,) if dc == DC - 1 else ())
            if sqa is None:
                sqa = t_sq
            if sq_eng == "act":
                opc("act", (lambda src_ap=src_ap, width=width, sqa=sqa:
                    nc.scalar.activation(
                        sqa[:, 0:DC * width], src_ap[:, 0:DC * width], AF.Square)),
                    reads=(srcname,), writes=(sqn,))
            else:
                opc("dve", (lambda src_ap=src_ap, width=width, sqa=sqa:
                    nc.vector.tensor_tensor(
                        sqa[:, 0:DC * width], src_ap[:, 0:DC * width],
                        src_ap[:, 0:DC * width], ALU.mult)),
                    reads=(srcname,), writes=(sqn,))
            for dc in range(DC):
                opc("pe", (lambda dc=dc, width=width, sqa=sqa, bB=bB:
                    nc.tensor.matmul(
                        bB[:, 0:width], t_onesb[:],
                        sqa[:, dc * width:(dc + 1) * width],
                        start=(dc == 0), stop=(dc == DC - 1))),
                    reads=(sqn, "onesb"),
                    wr_first=(bBn,) if dc == 0 else (),
                    writes=(bBn,) if dc == DC - 1 else ())
            opc("dve", (lambda mrow=mrow, width=width, bA=bA: nc.vector.tensor_scalar(
                mrow, bA[:, 0:width], INV_D, None, ALU.mult)),
                reads=(bAn, "consts"), writes=(mn,))
            opc("dve", (lambda mrow=mrow, vrow=vrow: nc.vector.tensor_tensor(
                vrow, mrow, mrow, ALU.mult)),
                reads=(mn,), writes=(vn,))
            opc("dve", (lambda vrow=vrow, width=width, bB=bB:
                nc.vector.scalar_tensor_tensor(
                    vrow, bB[:, 0:width], INV_D, vrow, ALU.mult, ALU.subtract)),
                reads=(bBn, "consts", vn), writes=(vn,))

        def rstd_of(vrow, rrow, vn, rn):
            opc("act", (lambda vrow=vrow: nc.scalar.activation(
                vrow, vrow, AF.Sqrt, bias=EPS_AP)),
                reads=(vn, "consts"), writes=(vn,))
            opc("dve", (lambda vrow=vrow, rrow=rrow: nc.vector.reciprocal(rrow, vrow)),
                reads=(vn,), writes=(rn,))

        def bcast(row_ap, rn, width, bank):
            bname = f"pb{bank}"
            opc("pe", (lambda row_ap=row_ap, width=width, bank=bank: nc.tensor.matmul(
                pb[bank][:, 0:width], t_onesr[:], row_ap, start=True, stop=True)),
                reads=("onesr", rn), wr_first=(bname,), writes=(bname,))

        for dst, src, n in ((t_onesb, d_onesb, "onesb"), (t_onesr, d_onesr, "onesr"),
                            (t_consts, d_consts, "consts"), (t_bu, d_bu, "bu"),
                            (t_bv, d_bv, "bv"), (t_bg1, d_bg1, "bg1"),
                            (t_bg2, d_bg2, "bg2"), (t_cb, d_cb, "cb"),
                            (t_cw, d_cw, "cw")):
            dma_in("sp", dst[:], src, n)
        em.scope("phA0")
        for t in range(NTA):
            xb, xbn = (t_xbf, "xbf") if t % 2 == 0 else (t_xbf2, "xbf2")
            dma_in("pool", xb, d_xfed[t], xbn)
            sl = slice(t * TA, (t + 1) * TA)
            if t % 2 == 0:
                bks = (prr[0], "pr0", prr[1], "pr1")
            else:
                bks = (pb[0][0:1, :], "pb0", pb[1][0:1, :], "pb1")
            if t == 5:
                dma_in("pool", t_wg1, d_wg1T, "wg1")
                dma_in("pool", t_wg2, d_wg2T, "wg2")
            elif t == NTA - 1:
                dma_in("pool", t_wuT, d_wuT, "wuT")
            row_stats(xb, xbn, TA, r_m[0:1, sl],
                      r_v[0:1, (t % 2) * TA:(t % 2) * TA + TA],
                      "rm", f"rv{t % 2}", banks=bks)
            rstd_of(r_v[0:1, (t % 2) * TA:(t % 2) * TA + TA], r_r[0:1, sl],
                    f"rv{t % 2}", "rr")

        em.barrier()

        em.scope("phA1")
        def a1_prep(t):
            xb, xbn = (t_xbf, "xbf") if t % 2 == 0 else (t_xbf2, "xbf2")
            sl = slice(t * TA, (t + 1) * TA)
            bcast(r_m[0:1, sl], "rm", TA, 4)
            bcast(r_r[0:1, sl], "rr", TA, 5)
            opc("dve", (lambda xb=xb: nc.vector.tensor_tensor(
                v3(xb, DC, TA), v3(xb, DC, TA), bc3(4, DC, TA), ALU.subtract)),
                reads=(xbn, "pb4"), writes=(xbn,))
            opc("dve", (lambda xb=xb: nc.vector.tensor_tensor(
                v3(xb, DC, TA), v3(xb, DC, TA), bc3(5, DC, TA), ALU.mult)),
                reads=(xbn, "pb5"), writes=(xbn,))

        dma_in("pool", t_xbf, d_xfed[0], "xbf")
        dma_in("sp", t_maskt[0], d_mask[0], "mask0")
        a1_prep(0)
        for t in range(NTA):
            owned = t >= NTA - 2
            ot = t - (NTA - 2)
            xb, xbn = (t_xbf, "xbf") if t % 2 == 0 else (t_xbf2, "xbf2")
            mk, mkn = t_maskt[t % 2], f"mask{t % 2}"
            if t + 1 < NTA:
                xb2, xbn2 = (t_xbf, "xbf") if (t + 1) % 2 == 0 else (t_xbf2, "xbf2")
                dma_in("pool", xb2, d_xfed[t + 1], xbn2)
                dma_in("sp", t_maskt[(t + 1) % 2], d_mask[t + 1],
                       f"mask{(t + 1) % 2}")
            for hc in range(HC):
                bk = 4 + hc % 2
                for dc in range(DC):
                    opc("pe", (lambda hc=hc, dc=dc, bk=bk, xb=xb: nc.tensor.matmul(
                        pb[bk][:, 0:TA],
                        t_wg1[:, (dc * HC + hc) * 128:(dc * HC + hc + 1) * 128],
                        xb[:, dc * TA:(dc + 1) * TA],
                        start=(dc == 0), stop=(dc == DC - 1))),
                        reads=("wg1", xbn),
                        wr_first=(f"pb{bk}",) if dc == 0 else (),
                        writes=(f"pb{bk}",) if dc == DC - 1 else ())
                opc("act", (lambda hc=hc, bk=bk: nc.scalar.activation(
                    t_h[:, hc * TA:(hc + 1) * TA], pb[bk][:, 0:TA], AF.Relu,
                    bias=t_bg1[:, hc:hc + 1])),
                    reads=(f"pb{bk}", "bg1"), writes=("hbf",))
            for rc in range(RC):
                bk = 2 + rc % 2
                for hc in range(HC):
                    opc("pe", (lambda rc=rc, hc=hc, bk=bk: nc.tensor.matmul(
                        pb[bk][:, 0:TA],
                        t_wg2[:, (hc * RC + rc) * 128:(hc * RC + rc + 1) * 128],
                        t_h[:, hc * TA:(hc + 1) * TA],
                        start=(hc == 0), stop=(hc == HC - 1))),
                        reads=("wg2", "hbf"),
                        wr_first=(f"pb{bk}",) if hc == 0 else (),
                        writes=(f"pb{bk}",) if hc == HC - 1 else ())
                opc("act", (lambda rc=rc, bk=bk: nc.scalar.activation(
                    t_g[:, rc * TA:(rc + 1) * TA], pb[bk][:, 0:TA], AF.Sigmoid,
                    bias=t_bg2[:, rc:rc + 1])),
                    reads=(f"pb{bk}", "bg2"), writes=("gbf",))
            if owned:
                dma_out("sp", d_g[:, :, ot * TA:(ot + 1) * TA],
                        t_g.rearrange("p (c t) -> p c t", c=RC), ("gbf",))
            if t + 1 < NTA:
                a1_prep(t + 1)
            for rc in range(RC):
                bk = rc % 4
                for dc in range(DC):
                    opc("pe", (lambda rc=rc, dc=dc, bk=bk, xb=xb: nc.tensor.matmul(
                        pb[bk][:, 0:TA],
                        t_wuT[:, (dc * RC + rc) * 128:(dc * RC + rc + 1) * 128],
                        xb[:, dc * TA:(dc + 1) * TA],
                        start=(dc == 0), stop=(dc == DC - 1))),
                        reads=("wuT", xbn),
                        wr_first=(f"pb{bk}",) if dc == 0 else (),
                        writes=(f"pb{bk}",) if dc == DC - 1 else ())
                opc("dve", (lambda rc=rc, bk=bk: nc.vector.scalar_tensor_tensor(
                    t_ug, pb[bk][:, 0:TA], t_bu[:, rc:rc + 1],
                    t_g[:, rc * TA:(rc + 1) * TA], ALU.add, ALU.mult)),
                    reads=(f"pb{bk}", "bu", "gbf"), writes=("ug",))
                sop, sname = t_so[rc % 3], f"so{rc % 3}"
                if t == 0:
                    opc("dve", (lambda sop=sop, mk=mk: nc.vector.tensor_tensor_scan(
                        sop, t_ug, mk, 0.0, ALU.add, ALU.mult)),
                        reads=("ug", mkn), writes=(sname,))
                else:
                    opc("dve", (lambda sop=sop, rc=rc, mk=mk:
                        nc.vector.tensor_tensor_scan(
                            sop, t_ug, mk, t_carry[:, rc:rc + 1],
                            ALU.add, ALU.mult)),
                        reads=("ug", mkn, "carry"), writes=(sname,))
                opc("dve", (lambda sop=sop, rc=rc: nc.vector.tensor_copy(
                    t_carry[:, rc:rc + 1], sop[:, TA - 1:TA])),
                    reads=(sname,), writes=("carry",))
                if owned:
                    dma_out("pool", d_kc[:, rc, ot * TA:(ot + 1) * TA], sop,
                            (sname,))

        em.barrier()

        em.scope("phB0")
        xo2 = d_xown.rearrange("p (c t) -> p c t", c=DC)
        xno3 = t_xnown.rearrange("p (c t) -> p c t", c=DC)
        for j in range(3):
            sl = slice(j * T0, (j + 1) * T0)
            xb, xbn = (t_xbf, "xbf") if j % 2 == 0 else (t_xbf2, "xbf2")
            dma_in("pool",
                   xb[:, 0:DC * T0].rearrange("p (c t) -> p c t", c=DC),
                   xo2[:, :, sl], xbn)
            row_stats(xb, xbn, T0, r_m[0:1, sl], r_v[0:1, 0:T0], "rm", "rv")
            rstd_of(r_v[0:1, 0:T0], r_r[0:1, sl], "rv", "rr")
            bcast(r_m[0:1, sl], "rm", T0, 4)
            bcast(r_r[0:1, sl], "rr", T0, 5)
            opc("dve", (lambda xb=xb: nc.vector.tensor_tensor(
                v3(xb[:, 0:DC * T0], DC, T0), v3(xb[:, 0:DC * T0], DC, T0),
                bc3(4, DC, T0), ALU.subtract)),
                reads=(xbn, "pb4"), writes=(xbn,))
            opc("dve", (lambda xb=xb, j=j: nc.vector.tensor_tensor(
                xno3[:, :, j * T0:(j + 1) * T0], v3(xb[:, 0:DC * T0], DC, T0),
                bc3(5, DC, T0), ALU.mult)),
                reads=(xbn, "pb5"), writes=("xnown",))

        em.barrier()

        em.scope("phB1")
        for rc in range(RC):
            wvb, wvn = t_wv[(rc // 2) % 2], f"wv{(rc // 2) % 2}"
            if rc % 2 == 0:
                dma_in("pool", wvb, d_wvT[rc // 2], wvn)
            kcb, kcn = t_kcs[rc % 2], f"kcs{rc % 2}"
            gsb, gsn = t_gss[rc % 2], f"gss{rc % 2}"
            dma_in("sp", kcb, d_kc[:, rc, :], kcn)
            dma_in("sp", gsb, d_g[:, rc, :], gsn)
            for half in range(2):
                bk = half
                for dc in range(DC):
                    opc("pe", (lambda half=half, dc=dc, bk=bk, wvb=wvb, wvn=wvn, rc=rc:
                        nc.tensor.matmul(
                            pb[bk][:, 0:512],
                            wvb[:, (rc % 2) * DC * 128 + dc * 128:
                                (rc % 2) * DC * 128 + (dc + 1) * 128],
                            t_xnown[:, dc * XO_W + 1 + half * 512:
                                    dc * XO_W + 1 + half * 512 + 512],
                            start=(dc == 0), stop=(dc == DC - 1))),
                        reads=(wvn, "xnown"),
                        wr_first=(f"pb{bk}",) if dc == 0 else (),
                        writes=(f"pb{bk}",) if dc == DC - 1 else ())
                opc("dve", (lambda rc=rc, half=half, bk=bk, gsb=gsb:
                    nc.vector.scalar_tensor_tensor(
                        t_vg[:, 0:512], pb[bk][:, 0:512], t_bv[:, rc:rc + 1],
                        gsb[:, half * 512: half * 512 + 512],
                        ALU.add, ALU.mult)),
                    reads=(f"pb{bk}", "bv", gsn), writes=("vg",))
                opc("dve", (lambda rc=rc, half=half, kcb=kcb: nc.vector.tensor_tensor(
                    t_go[:, rc * OWN + half * 512: rc * OWN + half * 512 + 512],
                    t_vg[:, 0:512], kcb[:, half * 512: half * 512 + 512],
                    ALU.mult)),
                    reads=("vg", kcn), writes=("go",))

        em.barrier()

        em.scope("phB2")

        def b2_compute(t):
            t_at, atn = t_attn[t % 2], f"attn{t % 2}"
            for dc in range(DC):
                cvb, cvn = t_cv[dc % 2], f"cv{dc % 2}"
                base = dc * XO_W + t * TB
                opc("act", (lambda base=base, dc=dc: nc.scalar.activation(
                    t_ca, t_xnown[:, base + 1: base + 1 + TB], AF.Identity,
                    scale=t_cw[:, dc * 3 + 1: dc * 3 + 2],
                    bias=t_cb[:, dc:dc + 1])),
                    reads=("xnown", "cw", "cb"), writes=("ca",))
                opc("act", (lambda base=base, dc=dc: nc.scalar.activation(
                    t_cb2, t_xnown[:, base + 2: base + 2 + TB], AF.Copy,
                    scale=t_cw[:, dc * 3 + 2: dc * 3 + 3])),
                    reads=("xnown", "cw"), writes=("cb2",))
                opc("dve", (lambda base=base, dc=dc: nc.vector.scalar_tensor_tensor(
                    t_ca, t_xnown[:, base: base + TB],
                    t_cw[:, dc * 3: dc * 3 + 1], t_ca, ALU.mult, ALU.add)),
                    reads=("xnown", "cw", "ca"), writes=("ca",))
                opc("dve", (lambda cvb=cvb: nc.vector.tensor_tensor(
                    cvb, t_ca, t_cb2, ALU.add)),
                    reads=("ca", "cb2"), writes=(cvn,))
                bk = dc % 2
                for rc in range(RC):
                    opc("pe", (lambda dc=dc, rc=rc, t=t, bk=bk: nc.tensor.matmul(
                        pb[bk][:, 0:TB],
                        t_wuT[:, (rc * RC + dc) * 128:(rc * RC + dc + 1) * 128],
                        t_go[:, rc * OWN + t * TB: rc * OWN + t * TB + TB],
                        start=(rc == 0), stop=(rc == RC - 1))),
                        reads=("wuT", "go"),
                        wr_first=(f"pb{bk}",) if rc == 0 else (),
                        writes=(f"pb{bk}",) if rc == RC - 1 else ())
                opc("dve", (lambda dc=dc, bk=bk, cvb=cvb, t_at=t_at:
                    nc.vector.tensor_tensor(
                        t_at[:, dc * TB:(dc + 1) * TB], pb[bk][:, 0:TB], cvb,
                        ALU.add)),
                    reads=(f"pb{bk}", cvn), writes=(atn,))
            if t % 2 == 0:
                bks = (prr[0], "pr0", prr[1], "pr1")
                rm, rv, rr = r_ms, r_vs, r_rs
                rmn, rvn, rrn = "rms", "rvs", "rrs"
            else:
                bks = (pb[2][0:1, :], "pb2", pb[3][0:1, :], "pb3")
                rm, rv, rr = r_ms2, r_vs2, r_rs2
                rmn, rvn, rrn = "rms2", "rvs2", "rrs2"
            row_stats(t_at, atn, TB, rm[0:1, 0:TB], rv[0:1, 0:TB],
                      rmn, rvn, sqa=t_sqb2, sqn="sqb2", banks=bks, sq_eng="act")
            rstd_of(rv[0:1, 0:TB], rr[0:1, 0:TB], rvn, rrn)

        def b2_final(t):
            t_at, atn = t_attn[t % 2], f"attn{t % 2}"
            if t % 2 == 0:
                rm, rr, rmn, rrn = r_ms, r_rs, "rms", "rrs"
            else:
                rm, rr, rmn, rrn = r_ms2, r_rs2, "rms2", "rrs2"
            bcast(rm[0:1, 0:TB], rmn, TB, 4)
            bcast(rr[0:1, 0:TB], rrn, TB, 5)
            opc("dve", (lambda t_at=t_at: nc.vector.tensor_tensor(
                v3(t_at, DC, TB), v3(t_at, DC, TB), bc3(4, DC, TB),
                ALU.subtract)),
                reads=(atn, "pb4"), writes=(atn,))
            opc("dve", (lambda t_at=t_at: nc.vector.tensor_tensor(
                v3(t_at, DC, TB), v3(t_at, DC, TB), bc3(5, DC, TB),
                ALU.mult)),
                reads=(atn, "pb5"), writes=(atn,))
            for hh in range(2):
                xofb, xofn = t_xof[hh], "xof0"
                xof3 = xofb.rearrange("p (c t) -> p c t", c=8)
                dma_in("sp", xof3,
                       xo2[:, hh * 8:(hh + 1) * 8, 1 + t * TB: 1 + t * TB + TB],
                       xofn)
                a3 = v3(t_at, DC, TB)[:, hh * 8:(hh + 1) * 8, :]
                opc("dve", (lambda xof3=xof3, a3=a3: nc.vector.tensor_tensor(
                    xof3, xof3, a3, ALU.add)),
                    reads=(xofn, atn), writes=(xofn,))
                dma_out("sp", d_x2[:, hh * 8:(hh + 1) * 8, t * TB: t * TB + TB],
                        xof3, (xofn,))

        for t in range(NTB):
            b2_compute(t)
            if t > 0:
                b2_final(t - 1)
        b2_final(NTB - 1)

        em.barrier()

        em.scope("phCP")

        def cp_stats(t):
            xb, xbn = (t_xbf, "xbf") if t % 2 == 0 else (t_xbf2, "xbf2")
            dma_in("pool", xb.rearrange("p (c t) -> p c t", c=DC),
                   d_x2[:, :, t * TC: t * TC + TC], xbn)
            bks = ((prr[0], "pr0", prr[1], "pr1") if t % 2 == 0 else
                   (pb[0][0:1, :], "pb0", pb[1][0:1, :], "pb1"))
            rm, rv, rr = (r_ms, r_vs, r_rs) if t % 2 == 0 else (r_ms2, r_vs2, r_rs2)
            rmn, rvn, rrn = (("rms", "rvs", "rrs") if t % 2 == 0 else
                             ("rms2", "rvs2", "rrs2"))
            row_stats(xb, xbn, TC, rm[0:1, 0:TC], rv[0:1, 0:TC], rmn, rvn,
                      banks=bks)
            rstd_of(rv[0:1, 0:TC], rr[0:1, 0:TC], rvn, rrn)

        def cp_apply(t):
            xb, xbn = (t_xbf, "xbf") if t % 2 == 0 else (t_xbf2, "xbf2")
            rm, rr = (r_ms, r_rs) if t % 2 == 0 else (r_ms2, r_rs2)
            rmn, rrn = ("rms", "rrs") if t % 2 == 0 else ("rms2", "rrs2")
            bb = (4, 5) if t % 2 == 0 else (2, 3)
            bcast(rm[0:1, 0:TC], rmn, TC, bb[0])
            bcast(rr[0:1, 0:TC], rrn, TC, bb[1])
            opc("dve", (lambda xb=xb, bb=bb, t=t: nc.vector.tensor_tensor(
                v3(t_xn2[t], DC, TC), v3(xb, DC, TC), bc3(bb[0], DC, TC),
                ALU.subtract)),
                reads=(xbn, f"pb{bb[0]}"), writes=(f"xn2{t}",))
            opc("dve", (lambda bb=bb, t=t: nc.vector.tensor_tensor(
                v3(t_xn2[t], DC, TC), v3(t_xn2[t], DC, TC), bc3(bb[1], DC, TC),
                ALU.mult)),
                reads=(f"xn2{t}", f"pb{bb[1]}"), writes=(f"xn2{t}",))

        cp_stats(0)
        cp_stats(1)
        cp_apply(0)
        cp_apply(1)

        r_ms2c = r_ms2

        em.barrier()

        em.scope("phCM")
        gcnt = [0]

        def next_bank():
            b = gcnt[0] % 6
            gcnt[0] += 1
            return b

        for hc in range(GC):
            wb, wn = t_wgu[hc % 2], f"wgu{hc % 2}"
            dma_in("pool", wb, d_wgu[hc], wn)
            for t in range(NTC):
                gbk = next_bank()
                ubk = next_bank()
                for dc in range(DC):
                    opc("pe", (lambda dc=dc, t=t, wb=wb, gbk=gbk: nc.tensor.matmul(
                        pb[gbk][:, 0:TC], wb[:, dc * 128:(dc + 1) * 128],
                        t_xn2[t][:, dc * TC:(dc + 1) * TC],
                        start=(dc == 0), stop=(dc == DC - 1))),
                        reads=(wn, f"xn2{t}"),
                        wr_first=(f"pb{gbk}",) if dc == 0 else (),
                        writes=(f"pb{gbk}",) if dc == DC - 1 else ())
                for dc in range(DC):
                    opc("pe", (lambda dc=dc, t=t, wb=wb, ubk=ubk: nc.tensor.matmul(
                        pb[ubk][:, 0:TC],
                        wb[:, DC * 128 + dc * 128: DC * 128 + (dc + 1) * 128],
                        t_xn2[t][:, dc * TC:(dc + 1) * TC],
                        start=(dc == 0), stop=(dc == DC - 1))),
                        reads=(wn, f"xn2{t}"),
                        wr_first=(f"pb{ubk}",) if dc == 0 else (),
                        writes=(f"pb{ubk}",) if dc == DC - 1 else ())
                opc("act", (lambda gbk=gbk: nc.scalar.activation(
                    t_sg, pb[gbk][:, 0:TC], AF.Silu)),
                    reads=(f"pb{gbk}",), writes=("sg",))
                if t == 0:
                    opc("dve", (lambda hc=hc, ubk=ubk: nc.vector.tensor_tensor(
                        t_gated[:, hc * TC:(hc + 1) * TC], t_sg,
                        pb[ubk][:, 0:TC], ALU.mult)),
                        reads=("sg", f"pb{ubk}"), writes=("gated",))
                else:
                    g1b, g1n = t_g1s[hc % 2], f"g1s{hc % 2}"
                    opc("dve", (lambda g1b=g1b, ubk=ubk: nc.vector.tensor_tensor(
                        g1b, t_sg, pb[ubk][:, 0:TC], ALU.mult)),
                        reads=("sg", f"pb{ubk}"), writes=(g1n,))
                    dma_out("sp", d_g1[:, hc * TC:(hc + 1) * TC], g1b, (g1n,))

        for t in range(NTC):
            for dc in range(DC):
                bk = 4 + dc % 2
                ring = [(t_wd[0], "wd0"), (t_wd[1], "wd1"),
                        (t_wgu[0], "wgu0"), (t_wgu[1], "wgu1")]
                for half in range(2):
                    wdb, wdn = ring[(dc * 2 + half) % 4]
                    dma_in("pool", wdb[:, 0:32 * 128],
                           d_wdT[dc, :, half * 32 * 128:(half + 1) * 32 * 128],
                           wdn)
                for half in range(2):
                    wdb, wdn = ring[(dc * 2 + half) % 4]
                    for k in range(32):
                        hcg = half * 32 + k
                        opc("pe", (lambda k=k, hcg=hcg, bk=bk, wdb=wdb:
                            nc.tensor.matmul(
                                pb[bk][:, 0:TC], wdb[:, k * 128:(k + 1) * 128],
                                t_gated[:, hcg * TC:(hcg + 1) * TC],
                                start=(hcg == 0), stop=(hcg == GC - 1))),
                            reads=(wdn, "gated"),
                            wr_first=(f"pb{bk}",) if hcg == 0 else (),
                            writes=(f"pb{bk}",) if hcg == GC - 1 else ())
                opc("act", (lambda dc=dc, bk=bk: nc.scalar.activation(
                    t_dbf[:, dc * TC:(dc + 1) * TC], pb[bk][:, 0:TC], AF.Copy)),
                    reads=(f"pb{bk}",), writes=("dbf",))
                opc("act", (lambda bk=bk: nc.scalar.activation(
                    t_sqc, pb[bk][:, 0:TC], AF.Square)),
                    reads=(f"pb{bk}",), writes=("sqc",))
                opc("pe", (lambda dc=dc: nc.tensor.matmul(
                    prr[0][:, 0:TC], t_onesb[:], t_dbf[:, dc * TC:(dc + 1) * TC],
                    start=(dc == 0), stop=(dc == DC - 1))),
                    reads=("dbf", "onesb"),
                    wr_first=("pr0",) if dc == 0 else (),
                    writes=("pr0",) if dc == DC - 1 else ())
                opc("pe", (lambda dc=dc: nc.tensor.matmul(
                    prr[1][:, 0:TC], t_onesb[:], t_sqc,
                    start=(dc == 0), stop=(dc == DC - 1))),
                    reads=("sqc", "onesb"),
                    wr_first=("pr1",) if dc == 0 else (),
                    writes=("pr1",) if dc == DC - 1 else ())
            if t == 0:
                dma_in("pool", t_gated, d_g1, "gated")
            opc("dve", (lambda: nc.vector.tensor_scalar(
                r_sr[0:1, 0:TC], prr[0][:, 0:TC], INV_D, None, ALU.mult)),
                reads=("pr0", "consts"), writes=("rsr",))
            opc("dve", (lambda: nc.vector.tensor_scalar(
                r_sr[0:1, TC:2 * TC], prr[1][:, 0:TC], INV_D, None, ALU.mult)),
                reads=("pr1", "consts"), writes=("rsr",))
            mm = r_sr[0:1, 0:TC]
            ee = r_sr[0:1, TC:2 * TC]
            opc("dve", (lambda mm=mm: nc.vector.tensor_tensor(
                r_vs[0:1, 0:TC], mm, mm, ALU.mult)),
                reads=("rsr",), writes=("rvs",))
            opc("dve", (lambda ee=ee: nc.vector.tensor_tensor(
                r_vs[0:1, 0:TC], ee, r_vs[0:1, 0:TC], ALU.subtract)),
                reads=("rsr", "rvs"), writes=("rvs",))
            rstd_of(r_vs[0:1, 0:TC], r_rs[0:1, 0:TC], "rvs", "rrs")
            bcast(mm, "rsr", TC, 2)
            bcast(r_rs[0:1, 0:TC], "rrs", TC, 3)
            opc("dve", (lambda: nc.vector.tensor_tensor(
                v3(t_dbf, DC, TC), v3(t_dbf, DC, TC), bc3(2, DC, TC),
                ALU.subtract)),
                reads=("dbf", "pb2"), writes=("dbf",))
            opc("dve", (lambda: nc.vector.tensor_tensor(
                v3(t_dbf, DC, TC), v3(t_dbf, DC, TC), bc3(3, DC, TC),
                ALU.mult)),
                reads=("dbf", "pb3"), writes=("dbf",))
            for hh in range(4):
                x2cb, x2cn = t_x2c[hh], "x2c0"
                x2c3 = x2cb.rearrange("p (c t) -> p c t", c=4)
                dma_in("sp", x2c3,
                       d_x2[:, hh * 4:(hh + 1) * 4, t * TC: t * TC + TC], x2cn)
                d3 = v3(t_dbf, DC, TC)[:, hh * 4:(hh + 1) * 4, :]
                opc("dve", (lambda x2c3=x2c3, d3=d3: nc.vector.tensor_tensor(
                    x2c3, x2c3, d3, ALU.add)),
                    reads=(x2cn, "dbf"), writes=(x2cn,))
                dma_out("sp", d_y[:, hh * 4:(hh + 1) * 4, t * TC: t * TC + TC],
                        x2c3, (x2cn,))

        em.barrier()

        em.finalize()
        with (
            nc.semaphore("s_pe") as s_pe,
            nc.semaphore("s_dve") as s_dve,
            nc.semaphore("s_act") as s_act,
            nc.semaphore("s_plc") as s_plc,
            nc.semaphore("s_ds") as s_ds,
            nc.semaphore("s_dp") as s_dp,
            nc.Block() as block,
        ):
            sems = {"pe": s_pe, "dve": s_dve, "act": s_act, "pool": s_plc,
                    "ds": s_ds, "dp": s_dp}

            def play(stream, e):
                open_scope = [None]
                for it in em.items[stream]:
                    if it[0] == "wait":
                        e.wait_ge(sems[it[1]], it[2])
                    elif it[0] == "op":
                        inst = it[1]()
                        if it[2] is not None:
                            inst.then_inc(sems[it[2]], 1)
                    elif it[0] == "scope":
                        if open_scope[0] is not None:
                            nc.leave_named_scope(open_scope[0][0], open_scope[0][1], False)
                        sid, _ = nc.enter_named_scope(it[1], False)
                        open_scope[0] = (it[1], sid)
                    else:
                        it[1]().then_inc(sems[it[2]], 16)
                if open_scope[0] is not None:
                    nc.leave_named_scope(open_scope[0][0], open_scope[0][1], False)

            @block.tensor
            def _(e):
                play("pe", e)

            @block.vector
            def _(e):
                play("dve", e)

            @block.scalar
            def _(e):
                play("act", e)

            @block.sync
            def _(e):
                play("sp", e)

            @block.gpsimd
            def _(e):
                play("pool", e)

    return nc


_NC_CACHE = {}


def kernel(**inputs):
    in_maps = make_in_maps(inputs)
    if "nc" not in _NC_CACHE:
        _NC_CACHE["nc"] = build_program()
    nc = _NC_CACHE["nc"]
    res = run_bass_kernel_spmd(nc, in_maps, core_ids=list(range(8)))
    out = np.zeros((B, L, D), dtype=np.float32)
    for c in range(8):
        b, p = c // 4, c % 4
        y = np.asarray(res.results[c]["y"])
        out[b, 1024 * p: 1024 * (p + 1), :] = (
            y.transpose(1, 0, 2).reshape(D, OWN).T)
    return out


# version 13
# speedup vs baseline: 1.1773x; 1.1773x over previous
import contextlib

import numpy as np
import ml_dtypes

import concourse.bass as bass
import concourse.mybir as mybir
from concourse.bass_utils import run_bass_kernel_spmd

F32 = mybir.dt.float32
BF16 = mybir.dt.bfloat16
AF = mybir.ActivationFunctionType
ALU = mybir.AluOpType
BF = ml_dtypes.bfloat16

B, L, D = 2, 4096, 2048
R, DH, HID = 2048, 512, 8192
DC, RC, HC, GC = 16, 16, 4, 64
W = 4096
TA = 512
NTA = W // TA
OWN = 1024
TB = 256
NTB = OWN // TB
TC = 512
NTC = OWN // TC
T0 = 342
XO_W = OWN + 2
EPS = 1e-5

A0, A1, B0, B1, B2, CP, CM = range(7)


class Em:
    STREAMS = ("pe", "dve", "act", "sp", "pool")

    def __init__(self):
        self.items = {s: [] for s in self.STREAMS}
        self.opcnt = {s: 0 for s in ("pe", "dve", "act", "pool")}
        self.dmacnt = {"ds": 0, "dp": 0}

    def _wait(self, stream, tok):
        if tok is None:
            return
        self.items[stream].append(("wait", tok[0], tok[1]))

    def op(self, stream, fn, waits=()):
        for t in waits:
            self._wait(stream, t)
        self.opcnt[stream] += 1
        v = self.opcnt[stream]
        self.items[stream].append(("op", fn, stream, v))
        return (stream, v)

    def dma(self, stream, fn, waits=()):
        for t in waits:
            self._wait(stream, t)
        sem = "ds" if stream == "sp" else "dp"
        self.dmacnt[sem] += 16
        v = self.dmacnt[sem]
        self.items[stream].append(("dma", fn, sem, v))
        return (sem, v)

    def scope(self, name):
        for stream in self.STREAMS:
            self.items[stream].append(("scope", name, 0))

    def barrier(self):
        toks = [(s, self.opcnt[s]) for s in ("pe", "dve", "act", "pool")]
        toks += [("ds", self.dmacnt["ds"]), ("dp", self.dmacnt["dp"])]
        for stream in self.STREAMS:
            for t in toks:
                self._wait(stream, t)

    def finalize(self):
        targets = {"pe": set(), "dve": set(), "act": set(), "pool": set()}
        for s in self.STREAMS:
            for it in self.items[s]:
                if it[0] == "wait" and it[1] in targets and it[2] > 0:
                    targets[it[1]].add(it[2])
        remap = {}
        for sem, tset in targets.items():
            marked = sorted(tset)
            remap[sem] = (set(marked), {v: i + 1 for i, v in enumerate(marked)})
        out = {s: [] for s in self.STREAMS}
        for s in self.STREAMS:
            last = {}
            for it in self.items[s]:
                if it[0] == "wait":
                    sem, val = it[1], it[2]
                    if sem in remap and val > 0:
                        val = remap[sem][1][val]
                    if last.get(sem, -1) >= val:
                        continue
                    last[sem] = val
                    out[s].append(("wait", sem, val))
                elif it[0] == "op":
                    _, fn, sem, v = it
                    inc = v in remap[sem][0]
                    out[s].append(("op", fn, sem if inc else None, 0))
                else:
                    out[s].append(it)
        self.items = out
        self._simulate()

    def _simulate(self):
        counts = {"pe": 0, "dve": 0, "act": 0, "pool": 0, "ds": 0, "dp": 0}
        pos = {s: 0 for s in self.STREAMS}
        progressed = True
        while progressed:
            progressed = False
            for s in self.STREAMS:
                while pos[s] < len(self.items[s]):
                    it = self.items[s][pos[s]]
                    if it[0] == "wait":
                        if counts[it[1]] >= it[2]:
                            pos[s] += 1
                            progressed = True
                        else:
                            break
                    elif it[0] == "op":
                        if it[2] is not None:
                            counts[it[2]] += 1
                        pos[s] += 1
                        progressed = True
                    elif it[0] == "scope":
                        pos[s] += 1
                        progressed = True
                    else:
                        counts[it[2]] = max(counts[it[2]], it[3])
                        pos[s] += 1
                        progressed = True
        stuck = {s: (pos[s], len(self.items[s]), self.items[s][pos[s]][:3])
                 for s in self.STREAMS if pos[s] < len(self.items[s])}
        if stuck:
            raise RuntimeError(f"emitter deadlock: {stuck} counts={counts}")


class Buf:
    def __init__(self):
        self.writer = None
        self.readers = []


class Slab:
    def __init__(self, nelem):
        self.cap = nelem
        self.placed = []
        self.peak = 0

    def alloc(self, nelem, phases, align=32):
        phases = set(phases)
        cands = {0}
        for off, size, ph in self.placed:
            cands.add(-(-(off + size) // align) * align)
        for off in sorted(cands):
            if off + nelem > self.cap:
                continue
            ok = True
            for o2, s2, ph in self.placed:
                if (ph & phases) and not (off + nelem <= o2 or o2 + s2 <= off):
                    ok = False
                    break
            if ok:
                self.placed.append((off, nelem, phases))
                self.peak = max(self.peak, off + nelem)
                return off
        raise RuntimeError(f"slab overflow: need {nelem} for phases {phases}")


def _tile_pd(mat, kc, ic):
    K, I = mat.shape
    assert K == kc * 128 and I == ic * 128
    return np.ascontiguousarray(
        mat.reshape(kc, 128, ic, 128).transpose(1, 0, 2, 3).reshape(128, kc * ic * 128))


def _stripes_i(mat, kc, ic):
    K, I = mat.shape
    assert K == kc * 128 and I == ic * 128
    return np.ascontiguousarray(
        mat.reshape(kc, 128, ic, 128).transpose(2, 1, 0, 3).reshape(ic, 128, kc * 128))


def _col(vec, n):
    return np.ascontiguousarray(np.asarray(vec, np.float32).reshape(n, 128).T)


def make_in_maps(inputs):
    x = np.asarray(inputs["L0_attn_x"], dtype=np.float32)
    w_gate = np.asarray(inputs["L0_ffn_w_gate"], dtype=np.float32)
    w_up = np.asarray(inputs["L0_ffn_w_up"], dtype=np.float32)
    conv_w = np.asarray(inputs["L0_attn_conv_w"], dtype=np.float32)

    wg_s = _stripes_i(np.ascontiguousarray(w_gate.T), DC, GC)
    wu_s = _stripes_i(np.ascontiguousarray(w_up.T), DC, GC)

    shared = dict(
        wuT=_tile_pd(np.ascontiguousarray(np.asarray(inputs["L0_attn_Wu"], np.float32).T), DC, RC),
        wg1T=_tile_pd(np.ascontiguousarray(np.asarray(inputs["L0_attn_Wg1"], np.float32).T), DC, HC),
        wg2T=_tile_pd(np.ascontiguousarray(np.asarray(inputs["L0_attn_Wg2"], np.float32).T), HC, RC),
        wvT=_stripes_i(np.ascontiguousarray(np.asarray(inputs["L0_attn_Wv"], np.float32).T), DC, RC).reshape(RC // 2, 2, 128, DC * 128).transpose(0, 2, 1, 3).reshape(RC // 2, 128, 2 * DC * 128),
        wgu=np.ascontiguousarray(np.stack([wg_s, wu_s], axis=1).transpose(0, 2, 1, 3).reshape(GC, 128, 2 * DC * 128)),
        wdT=_stripes_i(np.ascontiguousarray(np.asarray(inputs["L0_ffn_w_down"], np.float32).T), GC, DC),
        bu=_col(inputs["L0_attn_bu"], RC),
        bv=_col(inputs["L0_attn_bv"], RC),
        bg1=_col(inputs["L0_attn_bg1"], HC),
        bg2=_col(inputs["L0_attn_bg2"], RC),
        cb=_col(inputs["L0_attn_conv_b"], DC),
        cw=np.ascontiguousarray(
            conv_w.reshape(DC, 128, 3).transpose(1, 0, 2).reshape(128, DC * 3)),
        ones_bf=np.ones((128, 1), dtype=BF),
        ones_row=np.ones((1, 128), dtype=np.float32),
        ones_rbf=np.ones((1, 128), dtype=BF),
        consts=np.array([[1.0 / D, EPS, 0.0, 0.0]], dtype=np.float32),
    )

    in_maps = []
    for c in range(8):
        b, p = c // 4, c % 4
        xbT = np.ascontiguousarray(x[b].T)
        npad = 3072 - 1024 * p
        xfed = np.zeros((D, W), dtype=np.float32)
        if p > 0:
            xfed[:, npad:3072] = xbT[:, : 1024 * p]
        xfed[:, 3072:4096] = xbT[:, 1024 * p: 1024 * (p + 1)]
        xf_t = np.ascontiguousarray(
            xfed.reshape(DC, 128, NTA, TA).transpose(2, 1, 0, 3)
            .reshape(NTA, 128, DC * TA))
        mask = np.zeros((W,), dtype=np.float32)
        mask[npad:] = 1.0
        mask_t = np.ascontiguousarray(
            np.broadcast_to(mask.reshape(NTA, 1, TA), (NTA, 128, TA))).astype(BF)
        xo = np.zeros((D, XO_W), dtype=np.float32)
        xo[:, 1:1 + OWN] = xbT[:, 1024 * p: 1024 * (p + 1)]
        if p > 0:
            xo[:, 0] = xbT[:, 1024 * p - 1]
        if p < 3:
            xo[:, 1 + OWN] = xbT[:, 1024 * (p + 1)]
        xo_t = np.ascontiguousarray(
            xo.reshape(DC, 128, XO_W).transpose(1, 0, 2).reshape(128, DC * XO_W))
        m = dict(shared)
        m["xfed"] = xf_t
        m["mask"] = mask_t
        m["xown"] = xo_t
        in_maps.append(m)
    return in_maps


def build_program():
    nc = bass.Bass("TRN2", target_bir_lowering=False)

    d_xfed = nc.dram_tensor("xfed", [NTA, 128, DC * TA], F32, kind="ExternalInput").ap()
    d_mask = nc.dram_tensor("mask", [NTA, 128, TA], BF16, kind="ExternalInput").ap()
    d_xown = nc.dram_tensor("xown", [128, DC * XO_W], F32, kind="ExternalInput").ap()
    d_wuT = nc.dram_tensor("wuT", [128, DC * RC * 128], F32, kind="ExternalInput").ap()
    d_wg1T = nc.dram_tensor("wg1T", [128, DC * HC * 128], F32, kind="ExternalInput").ap()
    d_wg2T = nc.dram_tensor("wg2T", [128, HC * RC * 128], F32, kind="ExternalInput").ap()
    d_wvT = nc.dram_tensor("wvT", [RC // 2, 128, 2 * DC * 128], F32, kind="ExternalInput").ap()
    d_wgu = nc.dram_tensor("wgu", [GC, 128, 2 * DC * 128], F32, kind="ExternalInput").ap()
    d_wdT = nc.dram_tensor("wdT", [DC, 128, GC * 128], F32, kind="ExternalInput").ap()
    d_bu = nc.dram_tensor("bu", [128, RC], F32, kind="ExternalInput").ap()
    d_bv = nc.dram_tensor("bv", [128, RC], F32, kind="ExternalInput").ap()
    d_bg1 = nc.dram_tensor("bg1", [128, HC], F32, kind="ExternalInput").ap()
    d_bg2 = nc.dram_tensor("bg2", [128, RC], F32, kind="ExternalInput").ap()
    d_cb = nc.dram_tensor("cb", [128, DC], F32, kind="ExternalInput").ap()
    d_cw = nc.dram_tensor("cw", [128, DC * 3], F32, kind="ExternalInput").ap()
    d_onesb = nc.dram_tensor("ones_bf", [128, 1], BF16, kind="ExternalInput").ap()
    d_onesr = nc.dram_tensor("ones_row", [1, 128], F32, kind="ExternalInput").ap()
    d_onesrb = nc.dram_tensor("ones_rbf", [1, 128], BF16, kind="ExternalInput").ap()
    d_consts = nc.dram_tensor("consts", [1, 4], F32, kind="ExternalInput").ap()

    d_y = nc.dram_tensor("y", [128, DC, OWN], F32, kind="ExternalOutput").ap()
    d_x2 = nc.dram_tensor("x2d", [128, DC, OWN], F32, kind="Internal").ap()
    d_kc = nc.dram_tensor("kcd", [128, RC, OWN], BF16, kind="Internal").ap()
    d_g = nc.dram_tensor("gd", [128, RC, OWN], BF16, kind="Internal").ap()
    d_g1 = nc.dram_tensor("g1d", [128, GC * TC], BF16, kind="Internal").ap()

    em = Em()
    ctx = contextlib.ExitStack()
    with ctx:
        BF_CAP = 77900
        F32_CAP = 3392
        ROW_CAP = 9728
        t_arbf = ctx.enter_context(nc.sbuf_tensor("arbf", [128, BF_CAP], BF16))
        t_arf = ctx.enter_context(nc.sbuf_tensor("arf", [128, F32_CAP], F32))
        t_rows = ctx.enter_context(nc.sbuf_tensor("rows", [1, ROW_CAP], BF16))
        slab_bf, slab_f, slab_r = Slab(BF_CAP), Slab(F32_CAP), Slab(ROW_CAP)

        def abf(n, ph):
            off = slab_bf.alloc(n, ph)
            return t_arbf[:, off:off + n]

        def af32(n, ph):
            off = slab_f.alloc(n, ph)
            return t_arf[:, off:off + n]

        def arow(n, ph):
            off = slab_r.alloc(n, ph)
            return t_rows[0:1, off:off + n]

        sb = lambda name, shape, dt: ctx.enter_context(nc.sbuf_tensor(name, shape, dt))

        t_onesb = sb("c_onesb", [128, 1], BF16)
        t_onesr = sb("c_onesr", [1, 128], F32)
        t_onesrb = sb("c_onesrb", [1, 128], BF16)
        t_consts = sb("c_consts", [1, 4], F32)
        t_bu = sb("c_bu", [128, RC], F32)
        t_bv = sb("c_bv", [128, RC], F32)
        t_bg1 = sb("c_bg1", [128, HC], F32)
        t_bg2 = sb("c_bg2", [128, RC], F32)
        t_cb = sb("c_cb", [128, DC], F32)
        t_cw = sb("c_cw", [128, DC * 3], F32)
        t_carry = sb("c_carry", [128, RC], F32)

        t_wuT = abf(DC * RC * 128, {A0, A1, B0, B1, B2})
        t_xnown = abf(DC * XO_W, {B0, B1, B2})
        t_go = abf(RC * OWN, {B1, B2})
        t_wg1 = abf(DC * HC * 128, {A0, A1})
        t_wg2 = abf(HC * RC * 128, {A0, A1})
        t_xbf = abf(DC * TA, {A0, A1, B0, CP})
        t_xbf2 = abf(DC * TA, {A0, A1, B0, CP})
        t_g = abf(RC * TA, {A1})
        t_h = abf(HC * TA, {A1})
        t_maskt = [abf(TA, {A1}) for _ in range(2)]
        t_sq = abf(DC * TA, {A0, B0, CP})
        t_wv = [abf(2 * DC * 128, {B1}) for _ in range(2)]
        t_kcs = [abf(OWN, {B1}) for _ in range(2)]
        t_gss = [abf(OWN, {B1}) for _ in range(2)]
        t_attn = [abf(DC * TB, {B2}) for _ in range(2)]
        t_sqb2 = abf(DC * TB, {B2})
        t_xn2 = [abf(DC * TC, {CP, CM}) for _ in range(NTC)]
        t_gated = abf(GC * TC, {CM})
        t_wgu = [abf(2 * DC * 128, {CM}) for _ in range(2)]
        t_wd = [abf(32 * 128, {CM}) for _ in range(2)]
        t_g1s = [abf(TC, {CM}) for _ in range(2)]
        t_dbf = abf(DC * TC, {CM})
        t_sqc = abf(TC, {CM})

        t_ug = af32(TA, {A1})
        t_so = [af32(TA, {A1}) for _ in range(3)]
        t_vg = af32(TA, {B1})
        t_cv = [af32(TB, {B2}) for _ in range(2)]
        t_ca = af32(TB, {B2})
        t_cb2 = af32(TB, {B2})
        t_xof = [af32(8 * TB, {B2})] * 2
        t_sg = af32(TC, {CM})
        t_x2c = [af32(4 * TC, {CM})] * 4

        r_m = arow(W, {A0, A1, B0})
        r_r = arow(W, {A0, A1, B0})
        r_v = arow(2 * 512, {A0, B0})
        r_ms = arow(512, {B2, CP, CM})
        r_vs = arow(512, {B2, CP, CM})
        r_rs = arow(512, {B2, CP, CM})
        r_ms2 = arow(512, {B2, CP})
        r_vs2 = arow(512, {B2, CP})
        r_rs2 = arow(512, {B2, CP})
        r_sr = arow(2 * TC * NTC, {CP, CM})

        pb = [ctx.enter_context(nc.psum_tensor(f"pb{i}", [128, 512], F32))
              for i in range(6)]
        pr = [ctx.enter_context(nc.psum_tensor(f"pr{i}", [128, 512], F32))
              for i in range(2)]
        prr = [p[0:1, :] for p in pr]

        bufs = {}

        def Bf(name):
            if name not in bufs:
                bufs[name] = Buf()
            return bufs[name]

        def _rd(names):
            w = []
            for n in names:
                b = Bf(n)
                if b.writer:
                    w.append(b.writer)
            return w

        def _wr(names):
            w = []
            for n in names:
                b = Bf(n)
                w += b.readers
                if b.writer:
                    w.append(b.writer)
            return w

        def _post(tok, reads, writes):
            for n in reads:
                Bf(n).readers.append(tok)
            for n in writes:
                b = Bf(n)
                b.writer = tok
                b.readers = []

        def opc(stream, fn, reads=(), writes=(), wr_first=()):
            waits = _rd(reads) + _wr(writes) + _wr(wr_first)
            tok = em.op(stream, fn, waits=waits)
            _post(tok, reads, writes)
            return tok

        def dma_in(stream, dst_ap, src_ap, dstbuf):
            waits = _wr((dstbuf,))
            q = nc.sync if stream == "sp" else nc.gpsimd
            tok = em.dma(stream, lambda q=q, d=dst_ap, s=src_ap: q.dma_start(d, s),
                         waits=waits)
            _post(tok, (), (dstbuf,))
            return tok

        def dma_out(stream, dst_ap, src_ap, srcbufs):
            waits = _rd(srcbufs)
            q = nc.sync if stream == "sp" else nc.gpsimd
            tok = em.dma(stream, lambda q=q, d=dst_ap, s=src_ap: q.dma_start(d, s),
                         waits=waits)
            _post(tok, srcbufs, ())
            return tok

        def v3(ap, c, t):
            return ap.rearrange("p (c t) -> p c t", c=c)

        def bc3(bank, c, width):
            return pb[bank][:, 0:width].unsqueeze(1).broadcast_to((128, c, width))

        INV_D = t_consts[0:1, 0:1]
        EPS_AP = t_consts[0:1, 1:2]

        def row_stats(src_ap, srcname, width, mrow, vrow, mn, vn,
                      sqa=None, sqn="sq", banks=None, sq_eng="dve"):
            if banks is None:
                bA, bAn, bB, bBn = prr[0], "pr0", prr[1], "pr1"
            else:
                bA, bAn, bB, bBn = banks
            for dc in range(DC):
                opc("pe", (lambda dc=dc, src_ap=src_ap, width=width, bA=bA:
                    nc.tensor.matmul(
                        bA[:, 0:width], t_onesb[:],
                        src_ap[:, dc * width:(dc + 1) * width],
                        start=(dc == 0), stop=(dc == DC - 1))),
                    reads=(srcname, "onesb"),
                    wr_first=(bAn,) if dc == 0 else (),
                    writes=(bAn,) if dc == DC - 1 else ())
            if sqa is None:
                sqa = t_sq
            if sq_eng == "act":
                opc("act", (lambda src_ap=src_ap, width=width, sqa=sqa:
                    nc.scalar.activation(
                        sqa[:, 0:DC * width], src_ap[:, 0:DC * width], AF.Square)),
                    reads=(srcname,), writes=(sqn,))
            else:
                opc("dve", (lambda src_ap=src_ap, width=width, sqa=sqa:
                    nc.vector.tensor_tensor(
                        sqa[:, 0:DC * width], src_ap[:, 0:DC * width],
                        src_ap[:, 0:DC * width], ALU.mult)),
                    reads=(srcname,), writes=(sqn,))
            for dc in range(DC):
                opc("pe", (lambda dc=dc, width=width, sqa=sqa, bB=bB:
                    nc.tensor.matmul(
                        bB[:, 0:width], t_onesb[:],
                        sqa[:, dc * width:(dc + 1) * width],
                        start=(dc == 0), stop=(dc == DC - 1))),
                    reads=(sqn, "onesb"),
                    wr_first=(bBn,) if dc == 0 else (),
                    writes=(bBn,) if dc == DC - 1 else ())
            opc("dve", (lambda mrow=mrow, width=width, bA=bA: nc.vector.tensor_scalar(
                mrow, bA[:, 0:width], INV_D, None, ALU.mult)),
                reads=(bAn, "consts"), writes=(mn,))
            opc("dve", (lambda mrow=mrow, vrow=vrow: nc.vector.tensor_tensor(
                vrow, mrow, mrow, ALU.mult)),
                reads=(mn,), writes=(vn,))
            opc("dve", (lambda vrow=vrow, width=width, bB=bB:
                nc.vector.scalar_tensor_tensor(
                    vrow, bB[:, 0:width], INV_D, vrow, ALU.mult, ALU.subtract)),
                reads=(bBn, "consts", vn), writes=(vn,))

        def rstd_of(vrow, rrow, vn, rn):
            opc("act", (lambda vrow=vrow: nc.scalar.activation(
                vrow, vrow, AF.Sqrt, bias=EPS_AP)),
                reads=(vn, "consts"), writes=(vn,))
            opc("dve", (lambda vrow=vrow, rrow=rrow: nc.vector.reciprocal(rrow, vrow)),
                reads=(vn,), writes=(rn,))

        def bcast(row_ap, rn, width, bank):
            bname = f"pb{bank}"
            opc("pe", (lambda row_ap=row_ap, width=width, bank=bank: nc.tensor.matmul(
                pb[bank][:, 0:width], t_onesrb[:], row_ap, start=True, stop=True)),
                reads=("onesrb", rn), wr_first=(bname,), writes=(bname,))

        def warm(bank, n):
            for _ in range(n):
                opc("pe", (lambda bank=bank: nc.tensor.matmul(
                    pb[bank][0:1, 0:512], t_onesb[:], t_arbf[:, 0:512],
                    start=True, stop=True)),
                    reads=("onesb",), wr_first=(f"pb{bank}",),
                    writes=(f"pb{bank}",))

        for dst, src, n in ((t_onesb, d_onesb, "onesb"), (t_onesr, d_onesr, "onesr"),
                            (t_onesrb, d_onesrb, "onesrb"),
                            (t_consts, d_consts, "consts"), (t_bu, d_bu, "bu"),
                            (t_bv, d_bv, "bv"), (t_bg1, d_bg1, "bg1"),
                            (t_bg2, d_bg2, "bg2"), (t_cb, d_cb, "cb"),
                            (t_cw, d_cw, "cw")):
            dma_in("sp", dst[:], src, n)
        em.scope("phA0")
        for t in range(NTA):
            xb, xbn = (t_xbf, "xbf") if t % 2 == 0 else (t_xbf2, "xbf2")
            dma_in("pool", xb, d_xfed[t], xbn)
            sl = slice(t * TA, (t + 1) * TA)
            if t % 2 == 0:
                bks = (prr[0], "pr0", prr[1], "pr1")
            else:
                bks = (pb[0][0:1, :], "pb0", pb[1][0:1, :], "pb1")
            if t == 3:
                dma_in("pool", t_wg1, d_wg1T, "wg1")
                dma_in("pool", t_wg2, d_wg2T, "wg2")
            elif t >= 4:
                q = t - 4
                qn = DC * RC * 128 // 4
                dma_in("pool", t_wuT[:, q * qn:(q + 1) * qn],
                       d_wuT[:, q * qn:(q + 1) * qn], "wuT")
            row_stats(xb, xbn, TA, r_m[0:1, sl],
                      r_v[0:1, (t % 2) * TA:(t % 2) * TA + TA],
                      "rm", f"rv{t % 2}", banks=bks)
            rstd_of(r_v[0:1, (t % 2) * TA:(t % 2) * TA + TA], r_r[0:1, sl],
                    f"rv{t % 2}", "rr")

        em.barrier()

        em.scope("phA1")
        def a1_prep(t):
            xb, xbn = (t_xbf, "xbf") if t % 2 == 0 else (t_xbf2, "xbf2")
            sl = slice(t * TA, (t + 1) * TA)
            bcast(r_m[0:1, sl], "rm", TA, 4)
            bcast(r_r[0:1, sl], "rr", TA, 5)
            opc("dve", (lambda xb=xb: nc.vector.tensor_tensor(
                v3(xb, DC, TA), v3(xb, DC, TA), bc3(4, DC, TA), ALU.subtract)),
                reads=(xbn, "pb4"), writes=(xbn,))
            opc("dve", (lambda xb=xb: nc.vector.tensor_tensor(
                v3(xb, DC, TA), v3(xb, DC, TA), bc3(5, DC, TA), ALU.mult)),
                reads=(xbn, "pb5"), writes=(xbn,))

        dma_in("pool", t_xbf, d_xfed[0], "xbf")
        dma_in("sp", t_maskt[0], d_mask[0], "mask0")
        a1_prep(0)
        for t in range(NTA):
            owned = t >= NTA - 2
            ot = t - (NTA - 2)
            xb, xbn = (t_xbf, "xbf") if t % 2 == 0 else (t_xbf2, "xbf2")
            mk, mkn = t_maskt[t % 2], f"mask{t % 2}"
            if t + 1 < NTA:
                xb2, xbn2 = (t_xbf, "xbf") if (t + 1) % 2 == 0 else (t_xbf2, "xbf2")
                dma_in("pool", xb2, d_xfed[t + 1], xbn2)
                dma_in("sp", t_maskt[(t + 1) % 2], d_mask[t + 1],
                       f"mask{(t + 1) % 2}")
            for hc in range(HC):
                bk = 4 + hc % 2
                for dc in range(DC):
                    opc("pe", (lambda hc=hc, dc=dc, bk=bk, xb=xb: nc.tensor.matmul(
                        pb[bk][:, 0:TA],
                        t_wg1[:, (dc * HC + hc) * 128:(dc * HC + hc + 1) * 128],
                        xb[:, dc * TA:(dc + 1) * TA],
                        start=(dc == 0), stop=(dc == DC - 1))),
                        reads=("wg1", xbn),
                        wr_first=(f"pb{bk}",) if dc == 0 else (),
                        writes=(f"pb{bk}",) if dc == DC - 1 else ())
                opc("act", (lambda hc=hc, bk=bk: nc.scalar.activation(
                    t_h[:, hc * TA:(hc + 1) * TA], pb[bk][:, 0:TA], AF.Relu,
                    bias=t_bg1[:, hc:hc + 1])),
                    reads=(f"pb{bk}", "bg1"), writes=("hbf",))
            for rc in range(RC):
                bk = 2 + rc % 2
                for hc in range(HC):
                    opc("pe", (lambda rc=rc, hc=hc, bk=bk: nc.tensor.matmul(
                        pb[bk][:, 0:TA],
                        t_wg2[:, (hc * RC + rc) * 128:(hc * RC + rc + 1) * 128],
                        t_h[:, hc * TA:(hc + 1) * TA],
                        start=(hc == 0), stop=(hc == HC - 1))),
                        reads=("wg2", "hbf"),
                        wr_first=(f"pb{bk}",) if hc == 0 else (),
                        writes=(f"pb{bk}",) if hc == HC - 1 else ())
                opc("act", (lambda rc=rc, bk=bk: nc.scalar.activation(
                    t_g[:, rc * TA:(rc + 1) * TA], pb[bk][:, 0:TA], AF.Sigmoid,
                    bias=t_bg2[:, rc:rc + 1])),
                    reads=(f"pb{bk}", "bg2"), writes=("gbf",))
            if owned:
                dma_out("sp", d_g[:, :, ot * TA:(ot + 1) * TA],
                        t_g.rearrange("p (c t) -> p c t", c=RC), ("gbf",))
            if t + 1 < NTA:
                a1_prep(t + 1)
            for rc in range(RC):
                bk = rc % 4
                for dc in range(DC):
                    opc("pe", (lambda rc=rc, dc=dc, bk=bk, xb=xb: nc.tensor.matmul(
                        pb[bk][:, 0:TA],
                        t_wuT[:, (dc * RC + rc) * 128:(dc * RC + rc + 1) * 128],
                        xb[:, dc * TA:(dc + 1) * TA],
                        start=(dc == 0), stop=(dc == DC - 1))),
                        reads=("wuT", xbn),
                        wr_first=(f"pb{bk}",) if dc == 0 else (),
                        writes=(f"pb{bk}",) if dc == DC - 1 else ())
                opc("dve", (lambda rc=rc, bk=bk: nc.vector.scalar_tensor_tensor(
                    t_ug, pb[bk][:, 0:TA], t_bu[:, rc:rc + 1],
                    t_g[:, rc * TA:(rc + 1) * TA], ALU.add, ALU.mult)),
                    reads=(f"pb{bk}", "bu", "gbf"), writes=("ug",))
                sop, sname = t_so[rc % 3], f"so{rc % 3}"
                if t == 0:
                    opc("dve", (lambda sop=sop, mk=mk: nc.vector.tensor_tensor_scan(
                        sop, t_ug, mk, 0.0, ALU.add, ALU.mult)),
                        reads=("ug", mkn), writes=(sname,))
                else:
                    opc("dve", (lambda sop=sop, rc=rc, mk=mk:
                        nc.vector.tensor_tensor_scan(
                            sop, t_ug, mk, t_carry[:, rc:rc + 1],
                            ALU.add, ALU.mult)),
                        reads=("ug", mkn, "carry"), writes=(sname,))
                opc("dve", (lambda sop=sop, rc=rc: nc.vector.tensor_copy(
                    t_carry[:, rc:rc + 1], sop[:, TA - 1:TA])),
                    reads=(sname,), writes=("carry",))
                if owned:
                    dma_out("pool", d_kc[:, rc, ot * TA:(ot + 1) * TA], sop,
                            (sname,))

        em.barrier()

        em.scope("phB0")
        xo2 = d_xown.rearrange("p (c t) -> p c t", c=DC)
        xno3 = t_xnown.rearrange("p (c t) -> p c t", c=DC)
        for j in range(3):
            sl = slice(j * T0, (j + 1) * T0)
            xb, xbn = (t_xbf, "xbf") if j % 2 == 0 else (t_xbf2, "xbf2")
            dma_in("pool",
                   xb[:, 0:DC * T0].rearrange("p (c t) -> p c t", c=DC),
                   xo2[:, :, sl], xbn)
            row_stats(xb, xbn, T0, r_m[0:1, sl], r_v[0:1, 0:T0], "rm", "rv")
            rstd_of(r_v[0:1, 0:T0], r_r[0:1, sl], "rv", "rr")
            bcast(r_m[0:1, sl], "rm", T0, 4)
            bcast(r_r[0:1, sl], "rr", T0, 5)
            opc("dve", (lambda xb=xb: nc.vector.tensor_tensor(
                v3(xb[:, 0:DC * T0], DC, T0), v3(xb[:, 0:DC * T0], DC, T0),
                bc3(4, DC, T0), ALU.subtract)),
                reads=(xbn, "pb4"), writes=(xbn,))
            opc("dve", (lambda xb=xb, j=j: nc.vector.tensor_tensor(
                xno3[:, :, j * T0:(j + 1) * T0], v3(xb[:, 0:DC * T0], DC, T0),
                bc3(5, DC, T0), ALU.mult)),
                reads=(xbn, "pb5"), writes=("xnown",))

        em.barrier()

        em.scope("phB1")
        for rc in range(RC):
            wvb, wvn = t_wv[(rc // 2) % 2], f"wv{(rc // 2) % 2}"
            if rc % 2 == 0:
                dma_in("pool", wvb, d_wvT[rc // 2], wvn)
            kcb, kcn = t_kcs[rc % 2], f"kcs{rc % 2}"
            gsb, gsn = t_gss[rc % 2], f"gss{rc % 2}"
            dma_in("sp", kcb, d_kc[:, rc, :], kcn)
            dma_in("sp", gsb, d_g[:, rc, :], gsn)
            for half in range(2):
                bk = half
                for dc in range(DC):
                    opc("pe", (lambda half=half, dc=dc, bk=bk, wvb=wvb, wvn=wvn, rc=rc:
                        nc.tensor.matmul(
                            pb[bk][:, 0:512],
                            wvb[:, (rc % 2) * DC * 128 + dc * 128:
                                (rc % 2) * DC * 128 + (dc + 1) * 128],
                            t_xnown[:, dc * XO_W + 1 + half * 512:
                                    dc * XO_W + 1 + half * 512 + 512],
                            start=(dc == 0), stop=(dc == DC - 1))),
                        reads=(wvn, "xnown"),
                        wr_first=(f"pb{bk}",) if dc == 0 else (),
                        writes=(f"pb{bk}",) if dc == DC - 1 else ())
                opc("dve", (lambda rc=rc, half=half, bk=bk, gsb=gsb:
                    nc.vector.scalar_tensor_tensor(
                        t_vg[:, 0:512], pb[bk][:, 0:512], t_bv[:, rc:rc + 1],
                        gsb[:, half * 512: half * 512 + 512],
                        ALU.add, ALU.mult)),
                    reads=(f"pb{bk}", "bv", gsn), writes=("vg",))
                opc("dve", (lambda rc=rc, half=half, kcb=kcb: nc.vector.tensor_tensor(
                    t_go[:, rc * OWN + half * 512: rc * OWN + half * 512 + 512],
                    t_vg[:, 0:512], kcb[:, half * 512: half * 512 + 512],
                    ALU.mult)),
                    reads=("vg", kcn), writes=("go",))

        em.barrier()

        em.scope("phB2")

        def b2_compute(t):
            t_at, atn = t_attn[t % 2], f"attn{t % 2}"
            for dc in range(DC):
                cvb, cvn = t_cv[dc % 2], f"cv{dc % 2}"
                base = dc * XO_W + t * TB
                opc("act", (lambda base=base, dc=dc: nc.scalar.activation(
                    t_ca, t_xnown[:, base + 1: base + 1 + TB], AF.Identity,
                    scale=t_cw[:, dc * 3 + 1: dc * 3 + 2],
                    bias=t_cb[:, dc:dc + 1])),
                    reads=("xnown", "cw", "cb"), writes=("ca",))
                opc("act", (lambda base=base, dc=dc: nc.scalar.activation(
                    t_cb2, t_xnown[:, base + 2: base + 2 + TB], AF.Copy,
                    scale=t_cw[:, dc * 3 + 2: dc * 3 + 3])),
                    reads=("xnown", "cw"), writes=("cb2",))
                opc("dve", (lambda base=base, dc=dc: nc.vector.scalar_tensor_tensor(
                    t_ca, t_xnown[:, base: base + TB],
                    t_cw[:, dc * 3: dc * 3 + 1], t_ca, ALU.mult, ALU.add)),
                    reads=("xnown", "cw", "ca"), writes=("ca",))
                opc("dve", (lambda cvb=cvb: nc.vector.tensor_tensor(
                    cvb, t_ca, t_cb2, ALU.add)),
                    reads=("ca", "cb2"), writes=(cvn,))
                bk = dc % 2
                for rc in range(RC):
                    opc("pe", (lambda dc=dc, rc=rc, t=t, bk=bk: nc.tensor.matmul(
                        pb[bk][:, 0:TB],
                        t_wuT[:, (rc * RC + dc) * 128:(rc * RC + dc + 1) * 128],
                        t_go[:, rc * OWN + t * TB: rc * OWN + t * TB + TB],
                        start=(rc == 0), stop=(rc == RC - 1))),
                        reads=("wuT", "go"),
                        wr_first=(f"pb{bk}",) if rc == 0 else (),
                        writes=(f"pb{bk}",) if rc == RC - 1 else ())
                opc("dve", (lambda dc=dc, bk=bk, cvb=cvb, t_at=t_at:
                    nc.vector.tensor_tensor(
                        t_at[:, dc * TB:(dc + 1) * TB], pb[bk][:, 0:TB], cvb,
                        ALU.add)),
                    reads=(f"pb{bk}", cvn), writes=(atn,))
            if t % 2 == 0:
                bks = (prr[0], "pr0", prr[1], "pr1")
                rm, rv, rr = r_ms, r_vs, r_rs
                rmn, rvn, rrn = "rms", "rvs", "rrs"
            else:
                bks = (pb[2][0:1, :], "pb2", pb[3][0:1, :], "pb3")
                rm, rv, rr = r_ms2, r_vs2, r_rs2
                rmn, rvn, rrn = "rms2", "rvs2", "rrs2"
            row_stats(t_at, atn, TB, rm[0:1, 0:TB], rv[0:1, 0:TB],
                      rmn, rvn, sqa=t_sqb2, sqn="sqb2", banks=bks, sq_eng="act")
            rstd_of(rv[0:1, 0:TB], rr[0:1, 0:TB], rvn, rrn)

        def b2_final(t):
            t_at, atn = t_attn[t % 2], f"attn{t % 2}"
            if t % 2 == 0:
                rm, rr, rmn, rrn = r_ms, r_rs, "rms", "rrs"
            else:
                rm, rr, rmn, rrn = r_ms2, r_rs2, "rms2", "rrs2"
            bcast(rm[0:1, 0:TB], rmn, TB, 4)
            bcast(rr[0:1, 0:TB], rrn, TB, 5)
            opc("dve", (lambda t_at=t_at: nc.vector.tensor_tensor(
                v3(t_at, DC, TB), v3(t_at, DC, TB), bc3(4, DC, TB),
                ALU.subtract)),
                reads=(atn, "pb4"), writes=(atn,))
            opc("dve", (lambda t_at=t_at: nc.vector.tensor_tensor(
                v3(t_at, DC, TB), v3(t_at, DC, TB), bc3(5, DC, TB),
                ALU.mult)),
                reads=(atn, "pb5"), writes=(atn,))
            for hh in range(2):
                xofb, xofn = t_xof[hh], "xof0"
                xof3 = xofb.rearrange("p (c t) -> p c t", c=8)
                dma_in("sp", xof3,
                       xo2[:, hh * 8:(hh + 1) * 8, 1 + t * TB: 1 + t * TB + TB],
                       xofn)
                a3 = v3(t_at, DC, TB)[:, hh * 8:(hh + 1) * 8, :]
                opc("dve", (lambda xof3=xof3, a3=a3: nc.vector.tensor_tensor(
                    xof3, xof3, a3, ALU.add)),
                    reads=(xofn, atn), writes=(xofn,))
                dma_out("sp", d_x2[:, hh * 8:(hh + 1) * 8, t * TB: t * TB + TB],
                        xof3, (xofn,))

        for t in range(NTB):
            b2_compute(t)
            if t > 0:
                b2_final(t - 1)
        b2_final(NTB - 1)

        em.barrier()

        em.scope("phCP")

        def cp_stats(t):
            xb, xbn = (t_xbf, "xbf") if t % 2 == 0 else (t_xbf2, "xbf2")
            dma_in("pool", xb.rearrange("p (c t) -> p c t", c=DC),
                   d_x2[:, :, t * TC: t * TC + TC], xbn)
            bks = ((prr[0], "pr0", prr[1], "pr1") if t % 2 == 0 else
                   (pb[0][0:1, :], "pb0", pb[1][0:1, :], "pb1"))
            rm, rv, rr = (r_ms, r_vs, r_rs) if t % 2 == 0 else (r_ms2, r_vs2, r_rs2)
            rmn, rvn, rrn = (("rms", "rvs", "rrs") if t % 2 == 0 else
                             ("rms2", "rvs2", "rrs2"))
            row_stats(xb, xbn, TC, rm[0:1, 0:TC], rv[0:1, 0:TC], rmn, rvn,
                      banks=bks)
            rstd_of(rv[0:1, 0:TC], rr[0:1, 0:TC], rvn, rrn)

        def cp_apply(t):
            xb, xbn = (t_xbf, "xbf") if t % 2 == 0 else (t_xbf2, "xbf2")
            rm, rr = (r_ms, r_rs) if t % 2 == 0 else (r_ms2, r_rs2)
            rmn, rrn = ("rms", "rrs") if t % 2 == 0 else ("rms2", "rrs2")
            bb = (4, 5) if t % 2 == 0 else (2, 3)
            bcast(rm[0:1, 0:TC], rmn, TC, bb[0])
            bcast(rr[0:1, 0:TC], rrn, TC, bb[1])
            opc("dve", (lambda xb=xb, bb=bb, t=t: nc.vector.tensor_tensor(
                v3(t_xn2[t], DC, TC), v3(xb, DC, TC), bc3(bb[0], DC, TC),
                ALU.subtract)),
                reads=(xbn, f"pb{bb[0]}"), writes=(f"xn2{t}",))
            opc("dve", (lambda bb=bb, t=t: nc.vector.tensor_tensor(
                v3(t_xn2[t], DC, TC), v3(t_xn2[t], DC, TC), bc3(bb[1], DC, TC),
                ALU.mult)),
                reads=(f"xn2{t}", f"pb{bb[1]}"), writes=(f"xn2{t}",))

        cp_stats(0)
        cp_stats(1)
        cp_apply(0)
        cp_apply(1)

        r_ms2c = r_ms2

        em.barrier()

        em.scope("phCM")
        gcnt = [0]

        def next_bank():
            b = gcnt[0] % 6
            gcnt[0] += 1
            return b

        for hc in range(GC):
            wb, wn = t_wgu[hc % 2], f"wgu{hc % 2}"
            dma_in("pool", wb, d_wgu[hc], wn)
            for t in range(NTC):
                gbk = next_bank()
                ubk = next_bank()
                for dc in range(DC):
                    opc("pe", (lambda dc=dc, t=t, wb=wb, gbk=gbk: nc.tensor.matmul(
                        pb[gbk][:, 0:TC], wb[:, dc * 128:(dc + 1) * 128],
                        t_xn2[t][:, dc * TC:(dc + 1) * TC],
                        start=(dc == 0), stop=(dc == DC - 1))),
                        reads=(wn, f"xn2{t}"),
                        wr_first=(f"pb{gbk}",) if dc == 0 else (),
                        writes=(f"pb{gbk}",) if dc == DC - 1 else ())
                for dc in range(DC):
                    opc("pe", (lambda dc=dc, t=t, wb=wb, ubk=ubk: nc.tensor.matmul(
                        pb[ubk][:, 0:TC],
                        wb[:, DC * 128 + dc * 128: DC * 128 + (dc + 1) * 128],
                        t_xn2[t][:, dc * TC:(dc + 1) * TC],
                        start=(dc == 0), stop=(dc == DC - 1))),
                        reads=(wn, f"xn2{t}"),
                        wr_first=(f"pb{ubk}",) if dc == 0 else (),
                        writes=(f"pb{ubk}",) if dc == DC - 1 else ())
                opc("act", (lambda gbk=gbk: nc.scalar.activation(
                    t_sg, pb[gbk][:, 0:TC], AF.Silu)),
                    reads=(f"pb{gbk}",), writes=("sg",))
                if t == 0:
                    opc("dve", (lambda hc=hc, ubk=ubk: nc.vector.tensor_tensor(
                        t_gated[:, hc * TC:(hc + 1) * TC], t_sg,
                        pb[ubk][:, 0:TC], ALU.mult)),
                        reads=("sg", f"pb{ubk}"), writes=("gated",))
                else:
                    g1b, g1n = t_g1s[hc % 2], f"g1s{hc % 2}"
                    opc("dve", (lambda g1b=g1b, ubk=ubk: nc.vector.tensor_tensor(
                        g1b, t_sg, pb[ubk][:, 0:TC], ALU.mult)),
                        reads=("sg", f"pb{ubk}"), writes=(g1n,))
                    dma_out("sp", d_g1[:, hc * TC:(hc + 1) * TC], g1b, (g1n,))

        for t in range(NTC):
            for dc in range(DC):
                bk = 4 + dc % 2
                ring = [(t_wd[0], "wd0"), (t_wd[1], "wd1"),
                        (t_wgu[0], "wgu0"), (t_wgu[1], "wgu1")]
                for half in range(2):
                    wdb, wdn = ring[(dc * 2 + half) % 4]
                    dma_in("pool", wdb[:, 0:32 * 128],
                           d_wdT[dc, :, half * 32 * 128:(half + 1) * 32 * 128],
                           wdn)
                for half in range(2):
                    wdb, wdn = ring[(dc * 2 + half) % 4]
                    for k in range(32):
                        hcg = half * 32 + k
                        opc("pe", (lambda k=k, hcg=hcg, bk=bk, wdb=wdb:
                            nc.tensor.matmul(
                                pb[bk][:, 0:TC], wdb[:, k * 128:(k + 1) * 128],
                                t_gated[:, hcg * TC:(hcg + 1) * TC],
                                start=(hcg == 0), stop=(hcg == GC - 1))),
                            reads=(wdn, "gated"),
                            wr_first=(f"pb{bk}",) if hcg == 0 else (),
                            writes=(f"pb{bk}",) if hcg == GC - 1 else ())
                opc("act", (lambda dc=dc, bk=bk: nc.scalar.activation(
                    t_dbf[:, dc * TC:(dc + 1) * TC], pb[bk][:, 0:TC], AF.Copy)),
                    reads=(f"pb{bk}",), writes=("dbf",))
                opc("act", (lambda bk=bk: nc.scalar.activation(
                    t_sqc, pb[bk][:, 0:TC], AF.Square)),
                    reads=(f"pb{bk}",), writes=("sqc",))
                opc("pe", (lambda dc=dc: nc.tensor.matmul(
                    prr[0][:, 0:TC], t_onesb[:], t_dbf[:, dc * TC:(dc + 1) * TC],
                    start=(dc == 0), stop=(dc == DC - 1))),
                    reads=("dbf", "onesb"),
                    wr_first=("pr0",) if dc == 0 else (),
                    writes=("pr0",) if dc == DC - 1 else ())
                opc("pe", (lambda dc=dc: nc.tensor.matmul(
                    prr[1][:, 0:TC], t_onesb[:], t_sqc,
                    start=(dc == 0), stop=(dc == DC - 1))),
                    reads=("sqc", "onesb"),
                    wr_first=("pr1",) if dc == 0 else (),
                    writes=("pr1",) if dc == DC - 1 else ())
            if t == 0:
                dma_in("pool", t_gated, d_g1, "gated")
            opc("dve", (lambda: nc.vector.tensor_scalar(
                r_sr[0:1, 0:TC], prr[0][:, 0:TC], INV_D, None, ALU.mult)),
                reads=("pr0", "consts"), writes=("rsr",))
            opc("dve", (lambda: nc.vector.tensor_scalar(
                r_sr[0:1, TC:2 * TC], prr[1][:, 0:TC], INV_D, None, ALU.mult)),
                reads=("pr1", "consts"), writes=("rsr",))
            mm = r_sr[0:1, 0:TC]
            ee = r_sr[0:1, TC:2 * TC]
            opc("dve", (lambda mm=mm: nc.vector.tensor_tensor(
                r_vs[0:1, 0:TC], mm, mm, ALU.mult)),
                reads=("rsr",), writes=("rvs",))
            opc("dve", (lambda ee=ee: nc.vector.tensor_tensor(
                r_vs[0:1, 0:TC], ee, r_vs[0:1, 0:TC], ALU.subtract)),
                reads=("rsr", "rvs"), writes=("rvs",))
            rstd_of(r_vs[0:1, 0:TC], r_rs[0:1, 0:TC], "rvs", "rrs")
            bcast(mm, "rsr", TC, 2)
            bcast(r_rs[0:1, 0:TC], "rrs", TC, 3)
            opc("dve", (lambda: nc.vector.tensor_tensor(
                v3(t_dbf, DC, TC), v3(t_dbf, DC, TC), bc3(2, DC, TC),
                ALU.subtract)),
                reads=("dbf", "pb2"), writes=("dbf",))
            opc("dve", (lambda: nc.vector.tensor_tensor(
                v3(t_dbf, DC, TC), v3(t_dbf, DC, TC), bc3(3, DC, TC),
                ALU.mult)),
                reads=("dbf", "pb3"), writes=("dbf",))
            for hh in range(4):
                x2cb, x2cn = t_x2c[hh], "x2c0"
                x2c3 = x2cb.rearrange("p (c t) -> p c t", c=4)
                dma_in("sp", x2c3,
                       d_x2[:, hh * 4:(hh + 1) * 4, t * TC: t * TC + TC], x2cn)
                d3 = v3(t_dbf, DC, TC)[:, hh * 4:(hh + 1) * 4, :]
                opc("dve", (lambda x2c3=x2c3, d3=d3: nc.vector.tensor_tensor(
                    x2c3, x2c3, d3, ALU.add)),
                    reads=(x2cn, "dbf"), writes=(x2cn,))
                dma_out("sp", d_y[:, hh * 4:(hh + 1) * 4, t * TC: t * TC + TC],
                        x2c3, (x2cn,))

        em.barrier()

        em.finalize()
        with (
            nc.semaphore("s_pe") as s_pe,
            nc.semaphore("s_dve") as s_dve,
            nc.semaphore("s_act") as s_act,
            nc.semaphore("s_plc") as s_plc,
            nc.semaphore("s_ds") as s_ds,
            nc.semaphore("s_dp") as s_dp,
            nc.Block() as block,
        ):
            sems = {"pe": s_pe, "dve": s_dve, "act": s_act, "pool": s_plc,
                    "ds": s_ds, "dp": s_dp}

            def play(stream, e):
                open_scope = [None]
                for it in em.items[stream]:
                    if it[0] == "wait":
                        e.wait_ge(sems[it[1]], it[2])
                    elif it[0] == "op":
                        inst = it[1]()
                        if it[2] is not None:
                            inst.then_inc(sems[it[2]], 1)
                    elif it[0] == "scope":
                        if open_scope[0] is not None:
                            nc.leave_named_scope(open_scope[0][0], open_scope[0][1], False)
                        sid, _ = nc.enter_named_scope(it[1], False)
                        open_scope[0] = (it[1], sid)
                    else:
                        it[1]().then_inc(sems[it[2]], 16)
                if open_scope[0] is not None:
                    nc.leave_named_scope(open_scope[0][0], open_scope[0][1], False)

            lp = nc.allow_low_precision(
                reason="LN stat rows intentionally stored bf16; analysis in notes")

            @block.tensor
            def _(e):
                with lp if False else nc.allow_low_precision(reason="bf16 rows"):
                    play("pe", e)

            @block.vector
            def _(e):
                with nc.allow_low_precision(reason="bf16 rows"):
                    play("dve", e)

            @block.scalar
            def _(e):
                with nc.allow_low_precision(reason="bf16 rows"):
                    play("act", e)

            @block.sync
            def _(e):
                with nc.allow_low_precision(reason="bf16 rows"):
                    play("sp", e)

            @block.gpsimd
            def _(e):
                with nc.allow_low_precision(reason="bf16 rows"):
                    play("pool", e)

    return nc


_NC_CACHE = {}


def kernel(**inputs):
    in_maps = make_in_maps(inputs)
    if "nc" not in _NC_CACHE:
        _NC_CACHE["nc"] = build_program()
    nc = _NC_CACHE["nc"]
    res = run_bass_kernel_spmd(nc, in_maps, core_ids=list(range(8)))
    out = np.zeros((B, L, D), dtype=np.float32)
    for c in range(8):
        b, p = c // 4, c % 4
        y = np.asarray(res.results[c]["y"])
        out[b, 1024 * p: 1024 * (p + 1), :] = (
            y.transpose(1, 0, 2).reshape(D, OWN).T)
    return out
